# version 03
# baseline (speedup 1.0000x reference)
import numpy as np
import ml_dtypes
from contextlib import ExitStack

import concourse.bass as bass
import concourse.mybir as mybir
from concourse.bass_utils import run_bass_kernel_spmd

F32 = mybir.dt.float32
BF16 = mybir.dt.bfloat16
I32 = mybir.dt.int32
I8 = mybir.dt.int8
ALU = mybir.AluOpType
ACT = mybir.ActivationFunctionType

D = 4096
DC = 32
NH = 32
NKV = 8
SCALE = float(1.0 / np.sqrt(np.float32(128.0)))
EXP_OFF = -12.0
EPS = 1e-5
ESTRIP_W = 2176
MASK_W = 640
INV_D = 1.0 / 4096.0

ENGINES = ("sync", "tensor", "vector", "scalar", "gpsimd")

TAP = None
SYNC = 0


class Sem:
    __slots__ = ("h", "n")

    def __init__(self, h):
        self.h = h
        self.n = 0


class Plan:
    def __init__(self, nc):
        self.nc = nc
        self.ops = {e: [] for e in ENGINES}
        self.last_wait = {e: {} for e in ENGINES}

    def add(self, engine, builder, waits=(), inc=None, inc_by=1):
        lw = self.last_wait[engine]
        eff = []
        for (s, v) in waits:
            if v > 0 and lw.get(id(s), -1) < v:
                eff.append((s.h, v))
                lw[id(s)] = v
        if inc is not None:
            inc.n += inc_by

        def fn(eng, builder=builder, w=tuple(eff), inch=(inc.h if inc is not None else None), inc_by=inc_by):
            for (sh, v) in w:
                eng.wait_ge(sh, v)
            if builder is not None:
                ins = builder(eng)
                if inch is not None:
                    ins.then_inc(inch, inc_by)
            elif inch is not None:
                eng.nop().then_inc(inch, inc_by)

        self.ops[engine].append(fn)

    def flush(self):
        if not any(self.ops.values()):
            return
        with self.nc.Block() as block:
            for e in ENGINES:
                lst = self.ops[e]
                if not lst:
                    continue

                def mk(lst):
                    def body(eng):
                        for f in lst:
                            f(eng)
                    return body

                getattr(block, e)(mk(lst))
        self.ops = {e: [] for e in ENGINES}


def build_program(tap=None, sync_level=0):
    nc = bass.Bass("TRN2", target_bir_lowering=False)

    def din(name, shape, dt):
        return nc.dram_tensor(name, list(shape), dt, kind="ExternalInput").ap()

    x0 = din("x0", (D, 1024), F32)
    y = nc.dram_tensor("y", [D, 1024], F32, kind="ExternalOutput").ap()

    W = {}
    for L in (0, 1):
        W[L] = dict(
            wq=din(f"wq{L}", (32, 128, 4096), F32),
            wk=din(f"wk{L}", (8, 128, 4096), F32),
            wv=din(f"wv{L}", (4096, 1024), F32),
            wo=din(f"wo{L}", (8, 8, 128, 2048), F32),
            bq=din(f"bq{L}", (128, 32), F32),
            bk=din(f"bk{L}", (128, 8), F32),
            bo=din(f"bo{L}", (128, 32), F32),
            fb=din(f"fb{L}", (128, 32), F32),
            bv=din(f"bv{L}", (128, 1024), F32),
        )
    qt0 = din("qt0", (16, 128, 8192), I32)
    st0 = din("st0", (16, 128, 8192), F32)
    a1 = din("a1", (16, 128, 1024), I8)
    maskc = din("maskc", (128, MASK_W), BF16)
    estripc = din("estripc", (NH, 128, ESTRIP_W), BF16)
    identc = din("identc", (128, 128), BF16)

    cc_in = {L: nc.dram_tensor(f"cc_in{L}", [2048, 1024], BF16).ap() for L in (0, 1)}
    cc_out = {L: nc.dram_tensor(f"cc_out{L}", [4096, 1024], BF16).ap() for L in (0, 1)}

    with ExitStack() as st_outer:
        def mksem(name):
            return Sem(st_outer.enter_context(nc.semaphore(name)))

        s_t, s_v, s_sc, s_bar = [mksem(n) for n in ("t", "v", "sc", "bar")]
        s_cc = [mksem("cc0"), mksem("cc1")]
        d_ms = mksem("dms")
        d_x0 = mksem("dx0")
        d_bias = mksem("dbias")
        d_es = mksem("des")
        d_wk = [mksem(f"dwk{i}") for i in range(2)]
        d_wv = [mksem(f"dwv{i}") for i in range(2)]
        d_ks = [mksem(f"dks{i}") for i in range(2)]
        d_vs = [mksem(f"dvs{i}") for i in range(2)]
        d_st = [mksem(f"dst{i}") for i in range(2)]
        d_wq = [mksem(f"dwq{i}") for i in range(2)]
        d_wo = [mksem(f"dwo{i}") for i in range(2)]
        d_qt = [mksem(f"dqt{i}") for i in range(2)]
        d_a = [mksem(f"da{i}") for i in range(2)]
        d_y = [mksem(f"dy{i}") for i in range(2)]

        def sbo(name, shape, dt):
            return st_outer.enter_context(nc.sbuf_tensor(name, list(shape), dt))

        x_sb = sbo("x_sb", (128, DC * 1024), BF16)
        attn_buf = sbo("attn_buf", (128, 32 * 512), BF16)
        t_buf = sbo("t_buf", (128, 32 * 512), BF16)
        cpool = sbo("cpool", (128, 8), F32)
        ones128 = sbo("ones128", (128, 1), BF16)
        ones1 = sbo("ones1", (1, 128), F32)
        ident = sbo("ident", (128, 128), BF16)
        mask_sb = sbo("mask_sb", (128, MASK_W), BF16)
        bias_sb = sbo("bias_sb", (128, 104), F32)
        bv_sb = sbo("bv_sb", (128, 1024), F32)
        tsq_sb = sbo("tsq_sb", (128, 2 * 512), BF16)
        tmp_sb = sbo("tmp_sb", (128, 512), F32)
        ab_sb = sbo("ab_sb", (128, 1024), F32)
        bc_sb = sbo("bc_sb", (128, 512), F32)
        outst_sb = sbo("outst_sb", (128, 2 * 512), F32)
        ln_sb = sbo("ln_sb", (1, 6 * 512), F32)
        rl_sb = sbo("rl_sb", (1, 512), F32)

        for i, c in enumerate([0.0, 1.0, EXP_OFF, EPS, -0.5]):
            nc.const_aps.register(F32, c, cpool[:, i:i + 1])

        plan = Plan(nc)
        BQ, BK, BO, FB = 0, 32, 40, 72
        ysent = [0, 0]
        tsq_free = [0, 0]
        bias_cnt = [0]

        def barrier():
            tgt = s_bar.n + 5
            for e in ENGINES:
                plan.add(e, lambda eng: eng.nop(), inc=s_bar)
            assert s_bar.n == tgt
            for e in ENGINES:
                plan.add(e, None, waits=[(s_bar, tgt)])

        def psum_guard():
            return [(s_v, s_v.n), (s_sc, s_sc.n)]

        for i, c in enumerate([0.0, 1.0, EXP_OFF, EPS, -0.5]):
            plan.add("vector", lambda eng, i=i, c=c: eng.memset(cpool[:, i:i + 1], c))
        plan.add("vector", lambda eng: eng.memset(ones128[:], 1.0))
        plan.add("vector", lambda eng: eng.memset(ones1[:], 1.0))
        plan.add("sync", lambda eng: eng.dma_start(out=ident[:], in_=identc), inc=d_ms, inc_by=16)
        plan.add("sync", lambda eng: eng.dma_start(out=mask_sb[:], in_=maskc), inc=d_ms, inc_by=16)
        for c in range(DC):
            plan.add("gpsimd",
                     lambda eng, c=c: eng.dma_start(out=x_sb[:, c * 1024:(c + 1) * 1024],
                                                    in_=x0[c * 128:(c + 1) * 128, :]),
                     inc=d_x0, inc_by=16)
        barrier()
        plan.flush()

        def emit_ln(half, ps_st, st_off, ps_ab, subidx, tsrc):
            M, E2, MSQ, VAR, BP, RSTD = range(6)

            def lnc(i):
                return ln_sb[:, i * 512:(i + 1) * 512]

            tn = s_t.n
            plan.add("vector", lambda eng: eng.tensor_scalar_mul(
                lnc(M), ps_st[:, st_off:st_off + 512], INV_D), waits=[(s_t, tn)], inc=s_v)
            plan.add("vector", lambda eng: eng.tensor_scalar_mul(
                lnc(E2), ps_st[:, st_off + 512:st_off + 1024], INV_D), inc=s_v)
            plan.add("vector", lambda eng: eng.tensor_mul(lnc(MSQ), lnc(M), lnc(M)), inc=s_v)
            plan.add("vector", lambda eng: eng.tensor_sub(lnc(VAR), lnc(E2), lnc(MSQ)), inc=s_v)
            vn = s_v.n
            plan.add("scalar", lambda eng: eng.activation(lnc(BP), lnc(VAR), ACT.Ln, bias=EPS, scale=1.0),
                     waits=[(s_v, vn)], inc=s_sc)
            plan.add("scalar", lambda eng: eng.activation(lnc(RSTD), lnc(BP), ACT.Exp, bias=0.0, scale=-0.5),
                     inc=s_sc)
            scn = s_sc.n
            plan.add("vector", lambda eng: eng.tensor_mul(lnc(BP), lnc(M), lnc(RSTD)),
                     waits=[(s_sc, scn)], inc=s_v)
            vn = s_v.n
            plan.add("tensor", lambda eng: eng.matmul(
                ps_ab[:, 0:512], ones1[:], lnc(RSTD), start=True, stop=True),
                waits=[(s_v, vn)] + psum_guard())
            plan.add("tensor", lambda eng: eng.matmul(
                ps_ab[:, 512:1024], ones1[:], lnc(BP), start=True, stop=True), inc=s_t)
            tn = s_t.n
            gv = s_v.n
            plan.add("scalar", lambda eng: eng.copy(ab_sb[:, 0:512], ps_ab[:, 0:512]),
                     waits=[(s_t, tn), (s_v, gv)], inc=s_sc)
            plan.add("scalar", lambda eng: eng.copy(ab_sb[:, 512:1024], ps_ab[:, 512:1024]),
                     inc=s_sc)
            scn = s_sc.n

            write_y = (tap == subidx) if tap is not None else (subidx == 3)
            write_x = subidx != 3
            for oc in range(32):
                plan.add("vector", lambda eng, oc=oc:
                         eng.tensor_mul(tmp_sb[:], tsrc[:, oc * 512:(oc + 1) * 512],
                                        ab_sb[:, 0:512]),
                         waits=[(s_sc, scn)], inc=s_v)
                if write_x:
                    plan.add("vector", lambda eng, oc=oc, half=half:
                             eng.tensor_sub(x_sb[:, oc * 1024 + half * 512: oc * 1024 + half * 512 + 512],
                                            tmp_sb[:], ab_sb[:, 512:1024]),
                             inc=s_v)
                if write_y:
                    oslot = oc % 2
                    w = [(d_y[oslot], ysent[oslot])]
                    plan.add("vector", lambda eng, oslot=oslot:
                             eng.tensor_sub(outst_sb[:, oslot * 512:(oslot + 1) * 512],
                                            tmp_sb[:], ab_sb[:, 512:1024]),
                             waits=w, inc=s_v)
                    vn2 = s_v.n
                    plan.add("sync", lambda eng, oc=oc, oslot=oslot, half=half:
                             eng.dma_start(out=y[oc * 128:(oc + 1) * 128, half * 512:half * 512 + 512],
                                           in_=outst_sb[:, oslot * 512:(oslot + 1) * 512]),
                             waits=[(s_v, vn2)], inc=d_y[oslot], inc_by=16)
                    ysent[oslot] = d_y[oslot].n

        def emit_epilogue(ps_src, ps_sl, bias_col, oc, half, ps_st, st_off, first, last, tsrc):
            tn = s_t.n
            plan.add("vector", lambda eng, ps_src=ps_src, ps_sl=ps_sl, bias_col=bias_col, oc=oc, half=half, tsrc=tsrc:
                     eng.scalar_tensor_tensor(tsrc[:, oc * 512:(oc + 1) * 512],
                                              ps_src[:, ps_sl:ps_sl + 512],
                                              bias_sb[:, bias_col:bias_col + 1],
                                              x_sb[:, oc * 1024 + half * 512: oc * 1024 + half * 512 + 512],
                                              ALU.add, ALU.add),
                     waits=[(s_t, tn), (d_bias, bias_cnt[0])], inc=s_v)
            evac = s_v.n
            sl2 = oc % 2
            plan.add("vector", lambda eng, sl2=sl2, oc=oc, tsrc=tsrc:
                     eng.tensor_mul(tsq_sb[:, sl2 * 512:(sl2 + 1) * 512],
                                    tsrc[:, oc * 512:(oc + 1) * 512],
                                    tsrc[:, oc * 512:(oc + 1) * 512]),
                     waits=[(s_t, tsq_free[sl2])], inc=s_v)
            vn = s_v.n
            plan.add("tensor", lambda eng, oc=oc, tsrc=tsrc, ps_st=ps_st, st_off=st_off, first=first, last=last:
                     eng.matmul(ps_st[:, st_off:st_off + 512], ones128[:],
                                tsrc[:, oc * 512:(oc + 1) * 512], start=first, stop=last),
                     waits=[(s_v, vn)])
            plan.add("tensor", lambda eng, sl2=sl2, ps_st=ps_st, st_off=st_off, first=first, last=last:
                     eng.matmul(ps_st[:, st_off + 512:st_off + 1024], ones128[:],
                                tsq_sb[:, sl2 * 512:(sl2 + 1) * 512], start=first, stop=last),
                     inc=s_t)
            tsq_free[sl2] = s_t.n
            return evac

        for L in (0, 1):
            Wd = W[L]
            subidx_attn = 2 * L
            subidx_ffn = 2 * L + 1

            bw = [(s_v, s_v.n), (s_sc, s_sc.n), (s_t, s_t.n)]
            plan.add("sync", lambda eng, Wd=Wd: eng.dma_start(out=bias_sb[:, 0:32], in_=Wd["bq"]),
                     waits=bw, inc=d_bias, inc_by=16)
            plan.add("sync", lambda eng, Wd=Wd: eng.dma_start(out=bias_sb[:, 32:40], in_=Wd["bk"]),
                     inc=d_bias, inc_by=16)
            plan.add("sync", lambda eng, Wd=Wd: eng.dma_start(out=bias_sb[:, 40:72], in_=Wd["bo"]),
                     inc=d_bias, inc_by=16)
            plan.add("sync", lambda eng, Wd=Wd: eng.dma_start(out=bias_sb[:, 72:104], in_=Wd["fb"]),
                     inc=d_bias, inc_by=16)
            plan.add("sync", lambda eng, Wd=Wd: eng.dma_start(out=bv_sb[:], in_=Wd["bv"]),
                     inc=d_bias, inc_by=16)
            bias_cnt[0] = d_bias.n
            barrier()
            plan.flush()

            with ExitStack() as stp:
                wk_bf = stp.enter_context(nc.sbuf_tensor(f"wk_bf{L}", [128, 2 * 2048], BF16))
                kown = stp.enter_context(nc.sbuf_tensor(f"kown{L}", [128, 2 * 1024], BF16))
                ps_kv = stp.enter_context(nc.psum_tensor(f"ps_kv{L}", [128, 1024], F32))

                wk_used = [0, 0]
                kevac_last = 0
                ksent = [0, 0]
                for kv in range(NKV):
                    for ich in range(2):
                        slot = (kv * 2 + ich) % 2
                        plan.add("gpsimd",
                                 lambda eng, Wd=Wd, kv=kv, ich=ich, slot=slot:
                                 eng.dma_start(out=wk_bf[:, slot * 2048:(slot + 1) * 2048],
                                               in_=Wd["wk"][kv, :, ich * 2048:(ich + 1) * 2048]),
                                 waits=[(s_t, wk_used[slot])], inc=d_wk[slot], inc_by=16)
                        gpn = d_wk[slot].n
                        for icl in range(16):
                            ic = ich * 16 + icl
                            w = [(d_wk[slot], gpn)] if icl == 0 else []
                            if ic == 0:
                                w += psum_guard() if kv == 0 else [(s_v, kevac_last)]
                                if kv == 0:
                                    w.append((d_x0, d_x0.n))
                            plan.add("tensor",
                                     lambda eng, ic=ic, slot=slot, icl=icl:
                                     eng.matmul(ps_kv[:, 0:512],
                                                wk_bf[:, slot * 2048 + icl * 128: slot * 2048 + (icl + 1) * 128],
                                                x_sb[:, ic * 1024: ic * 1024 + 512],
                                                start=(ic == 0), stop=(ic == 31)),
                                     waits=w)
                            plan.add("tensor",
                                     lambda eng, ic=ic, slot=slot, icl=icl:
                                     eng.matmul(ps_kv[:, 512:1024],
                                                wk_bf[:, slot * 2048 + icl * 128: slot * 2048 + (icl + 1) * 128],
                                                x_sb[:, ic * 1024 + 512: ic * 1024 + 1024],
                                                start=(ic == 0), stop=(ic == 31)),
                                     inc=(s_t if icl == 15 else None))
                        wk_used[slot] = s_t.n
                    tn = s_t.n
                    kslot = kv % 2
                    plan.add("vector",
                             lambda eng, kv=kv, kslot=kslot:
                             eng.tensor_scalar_add(kown[:, kslot * 1024:(kslot + 1) * 1024],
                                                   ps_kv[:], bias_sb[:, BK + kv:BK + kv + 1]),
                             waits=[(s_t, tn), (d_ks[kslot], ksent[kslot]),
                                    (d_bias, bias_cnt[0])], inc=s_v)
                    kevac_last = s_v.n
                    vn = s_v.n
                    plan.add("sync",
                             lambda eng, L=L, kv=kv, kslot=kslot:
                             eng.dma_start(out=cc_in[L][kv * 128:(kv + 1) * 128, :],
                                           in_=kown[:, kslot * 1024:(kslot + 1) * 1024]),
                             waits=[(s_v, vn)], inc=d_ks[kslot], inc_by=16)
                    ksent[kslot] = d_ks[kslot].n
                plan.flush()

            with ExitStack() as stp:
                wv_bf = stp.enter_context(nc.sbuf_tensor(f"wv_bf{L}", [128, 2 * 512], BF16))
                vown = stp.enter_context(nc.sbuf_tensor(f"vown{L}", [128, 2 * 512], BF16))
                ps_v = stp.enter_context(nc.psum_tensor(f"ps_v{L}", [128, 4096], F32))

                wvu = [0, 0]
                vsent = [0, 0]
                for fh in range(2):
                    for ic in range(32):
                        slot = ic % 2
                        w = [(s_t, wvu[slot])]
                        if fh == 1 and ic == 0:
                            w.append((s_v, s_v.n))
                        plan.add("gpsimd",
                                 lambda eng, Wd=Wd, fh=fh, ic=ic, slot=slot:
                                 eng.dma_start(out=wv_bf[:, slot * 512:(slot + 1) * 512],
                                               in_=Wd["wv"][ic * 128:(ic + 1) * 128, fh * 512:(fh + 1) * 512]),
                                 waits=w, inc=d_wv[slot], inc_by=16)
                        gpn = d_wv[slot].n
                        for tc in range(8):
                            w = [(d_wv[slot], gpn)] if tc == 0 else []
                            if fh == 0 and ic == 0 and tc == 0:
                                w += psum_guard()
                            plan.add("tensor",
                                     lambda eng, ic=ic, tc=tc, slot=slot:
                                     eng.matmul(ps_v[:, tc * 512:(tc + 1) * 512],
                                                x_sb[:, ic * 1024 + tc * 128: ic * 1024 + (tc + 1) * 128],
                                                wv_bf[:, slot * 512:(slot + 1) * 512],
                                                start=(ic == 0), stop=(ic == 31)),
                                     waits=w, inc=(s_t if tc == 7 else None))
                        wvu[slot] = s_t.n
                    tn = s_t.n
                    for tc in range(8):
                        vslot = tc % 2
                        w = [(s_t, tn), (d_vs[vslot], vsent[vslot]),
                             (d_bias, bias_cnt[0])]
                        plan.add("vector",
                                 lambda eng, fh=fh, tc=tc, vslot=vslot:
                                 eng.tensor_tensor(vown[:, vslot * 512:(vslot + 1) * 512],
                                                   ps_v[:, tc * 512:(tc + 1) * 512],
                                                   bv_sb[:, fh * 512:(fh + 1) * 512], ALU.add),
                                 waits=w, inc=s_v)
                        vn = s_v.n
                        plan.add("sync",
                                 lambda eng, L=L, fh=fh, tc=tc, vslot=vslot:
                                 eng.dma_start(out=cc_in[L][1024 + tc * 128: 1024 + (tc + 1) * 128,
                                                            fh * 512:(fh + 1) * 512],
                                               in_=vown[:, vslot * 512:(vslot + 1) * 512]),
                                 waits=[(s_v, vn)], inc=d_vs[vslot], inc_by=16)
                        vsent[vslot] = d_vs[vslot].n
                ccw = [(d_ks[0], d_ks[0].n), (d_ks[1], d_ks[1].n),
                       (d_vs[0], d_vs[0].n), (d_vs[1], d_vs[1].n)]
                plan.add("gpsimd",
                         lambda eng, L=L:
                         eng.collective_compute("AllGather", ALU.bypass,
                                                replica_groups=[[0, 1], [2, 3], [4, 5], [6, 7]],
                                                ins=[cc_in[L][:, :]], outs=[cc_out[L][:, :]]),
                         waits=ccw, inc=s_cc[L])
                plan.flush()

            with ExitStack() as stp:
                kt = stp.enter_context(nc.sbuf_tensor(f"kt{L}", [128, 2 * 2048], BF16))
                vt = stp.enter_context(nc.sbuf_tensor(f"vt{L}", [128, 2 * 2048], BF16))
                q_bf = stp.enter_context(nc.sbuf_tensor(f"q_bf{L}", [128, 2 * 512], BF16))
                pt = stp.enter_context(nc.sbuf_tensor(f"pt{L}", [128, 2 * 1024], BF16))
                et = stp.enter_context(nc.sbuf_tensor(f"et{L}", [128, 1024], BF16))
                wq_bf = stp.enter_context(nc.sbuf_tensor(f"wq_bf{L}", [128, 2 * 1024], BF16))
                wo_bf = stp.enter_context(nc.sbuf_tensor(f"wo_bf{L}", [128, 2 * 2048], BF16))
                estrip = (stp.enter_context(nc.sbuf_tensor("estrip1", [128, ESTRIP_W], BF16))
                          if L == 1 else None)

                stitch_done = [0, 0]
                wq_used = [0, 0]
                est_used = 0
                for H in range(2):
                    with ExitStack() as stq:
                        ps_q = stq.enter_context(nc.psum_tensor(f"ps_q{L}{H}", [128, 512], F32))
                        ps_s = [stq.enter_context(nc.psum_tensor(f"ps_s{L}{H}{r}", [128, 1024], F32))
                                for r in range(2)]
                        ps_o = stq.enter_context(nc.psum_tensor(f"ps_o{L}{H}", [128, 512], F32))
                        ps_l = stq.enter_context(nc.psum_tensor(f"ps_l{L}{H}", [1, 512], F32))
                        ps_bc = stq.enter_context(nc.psum_tensor(f"ps_bc{L}{H}", [128, 512], F32))

                        sgrp_exp = [0, 0]
                        pt_done = [0, 0]
                        et_free = 0
                        ring = 0
                        ptring = 0
                        head_done_v = 0
                        bc_read = 0
                        bc_mm = 0
                        qev_last = 0
                        qev = [0, 0]
                        qread = [0, 0]
                        guard = psum_guard()

                        for kv in range(NKV):
                            kslot = kv % 2
                            w = [(s_cc[L], 1), (s_t, stitch_done[kslot])]
                            for s in range(8):
                                plan.add("sync",
                                         lambda eng, L=L, kv=kv, s=s, kslot=kslot:
                                         eng.dma_start(
                                             out=kt[:, kslot * 2048 + s * 256: kslot * 2048 + (s + 1) * 256],
                                             in_=cc_out[L][2048 * (s % 2) + kv * 128:
                                                           2048 * (s % 2) + (kv + 1) * 128,
                                                           256 * (s // 2): 256 * (s // 2) + 256]),
                                         waits=w, inc=d_st[kslot], inc_by=16)
                                w = []
                            for tc in range(16):
                                s = tc // 2
                                ro = 2048 * (s % 2) + 1024 + 256 * (s // 2) + 128 * (tc % 2)
                                plan.add("sync",
                                         lambda eng, L=L, kv=kv, tc=tc, ro=ro, kslot=kslot:
                                         eng.dma_start(
                                             out=vt[:, kslot * 2048 + tc * 128: kslot * 2048 + (tc + 1) * 128],
                                             in_=cc_out[L][ro:ro + 128, kv * 128:(kv + 1) * 128]),
                                         inc=d_st[kslot], inc_by=16)
                            stitch_n = d_st[kslot].n

                            for g in range(4):
                                h = kv * 4 + g
                                hslot = h % 2
                                if L == 1:
                                    plan.add("sync",
                                             lambda eng, h=h:
                                             eng.dma_start(out=estrip[:, :],
                                                           in_=estripc[h, :, :]),
                                             waits=[(s_v, est_used)], inc=d_es, inc_by=16)
                                estrip_n = d_es.n

                                for ich in range(4):
                                    wqs = ich % 2
                                    plan.add("gpsimd",
                                             lambda eng, Wd=Wd, h=h, ich=ich, wqs=wqs:
                                             eng.dma_start(out=wq_bf[:, wqs * 1024:(wqs + 1) * 1024],
                                                           in_=Wd["wq"][h, :, ich * 1024:(ich + 1) * 1024]),
                                             waits=[(s_t, wq_used[wqs])], inc=d_wq[wqs], inc_by=16)
                                    gpn = d_wq[wqs].n
                                    for icl in range(8):
                                        ic = ich * 8 + icl
                                        w = [(d_wq[wqs], gpn)] if icl == 0 else []
                                        if ic == 0:
                                            w += guard
                                            w.append((s_v, qev_last))
                                        plan.add("tensor",
                                                 lambda eng, ic=ic, wqs=wqs, icl=icl, H=H:
                                                 eng.matmul(ps_q[:],
                                                            wq_bf[:, wqs * 1024 + icl * 128: wqs * 1024 + (icl + 1) * 128],
                                                            x_sb[:, ic * 1024 + H * 512: ic * 1024 + H * 512 + 512],
                                                            start=(ic == 0), stop=(ic == 31)),
                                                 waits=w, inc=(s_t if icl == 7 else None))
                                    wq_used[wqs] = s_t.n
                                qmm_n = s_t.n
                                plan.add("vector",
                                         lambda eng, h=h, hslot=hslot:
                                         eng.tensor_scalar(q_bf[:, hslot * 512:(hslot + 1) * 512],
                                                           ps_q[:],
                                                           bias_sb[:, BQ + h:BQ + h + 1],
                                                           SCALE, ALU.add, ALU.mult),
                                         waits=[(s_t, max(qmm_n, qread[hslot])),
                                                (d_bias, bias_cnt[0])], inc=s_v)
                                qev_last = s_v.n
                                qev[hslot] = s_v.n

                                for il in range(2):
                                    i = 2 * H + il
                                    for grp in range(i + 1):
                                        r = ring
                                        ring ^= 1
                                        for j in range(4):
                                            kc = 4 * grp + j
                                            w = []
                                            if j == 0:
                                                w = [(s_sc, sgrp_exp[r]), (s_v, qev[hslot]),
                                                     (d_st[kslot], stitch_n)]
                                            plan.add("tensor",
                                                     lambda eng, kc=kc, j=j, r=r, kslot=kslot, hslot=hslot, il=il:
                                                     eng.matmul(ps_s[r][:, j * 256:(j + 1) * 256],
                                                                kt[:, kslot * 2048 + kc * 128: kslot * 2048 + (kc + 1) * 128],
                                                                q_bf[:, hslot * 512 + il * 256: hslot * 512 + (il + 1) * 256],
                                                                start=True, stop=True),
                                                     waits=w, inc=(s_t if j == 3 else None))
                                        smm_n = s_t.n
                                        masked = (L == 1) or (grp == i)
                                        pr = ptring
                                        ptring ^= 1
                                        if masked:
                                            plan.add("scalar",
                                                     lambda eng, r=r:
                                                     eng.activation(et[:], ps_s[r][:], ACT.Exp,
                                                                    bias=EXP_OFF, scale=1.0),
                                                     waits=[(s_t, smm_n), (s_v, et_free)], inc=s_sc)
                                            exp_n = s_sc.n
                                            sgrp_exp[r] = exp_n
                                            for j in range(4):
                                                w = [(s_sc, exp_n)]
                                                if j == 0:
                                                    w.append((s_t, pt_done[pr]))
                                                    if L == 1:
                                                        w.append((d_es, estrip_n))
                                                    else:
                                                        w.append((d_ms, d_ms.n))
                                                if L == 0:
                                                    def mkb(j=j, pr=pr):
                                                        def b(eng):
                                                            return eng.tensor_mul(
                                                                pt[:, pr * 1024 + j * 256: pr * 1024 + (j + 1) * 256],
                                                                et[:, j * 256:(j + 1) * 256],
                                                                mask_sb[:, 384 - 128 * j: 640 - 128 * j])
                                                        return b
                                                else:
                                                    m = 512 * (i - grp) - 128 * j + 384

                                                    def mkb(j=j, pr=pr, m=m):
                                                        def b(eng):
                                                            return eng.tensor_mul(
                                                                pt[:, pr * 1024 + j * 256: pr * 1024 + (j + 1) * 256],
                                                                et[:, j * 256:(j + 1) * 256],
                                                                estrip[:, m: m + 256])
                                                        return b
                                                plan.add("vector", mkb(), waits=w,
                                                         inc=(s_v if j == 3 else None))
                                            et_free = s_v.n
                                            pt_ready = (s_v, s_v.n)
                                        else:
                                            plan.add("scalar",
                                                     lambda eng, r=r, pr=pr:
                                                     eng.activation(pt[:, pr * 1024:(pr + 1) * 1024],
                                                                    ps_s[r][:], ACT.Exp,
                                                                    bias=EXP_OFF, scale=1.0),
                                                     waits=[(s_t, smm_n), (s_t, pt_done[pr])], inc=s_sc)
                                            sgrp_exp[r] = s_sc.n
                                            pt_ready = (s_sc, s_sc.n)
                                        for j in range(4):
                                            kc = 4 * grp + j
                                            first = (kc == 0)
                                            last = (kc == 4 * i + 3)
                                            w = []
                                            if j == 0:
                                                w = [pt_ready]
                                                if first and grp == 0:
                                                    w.append((s_v, head_done_v))
                                            plan.add("tensor",
                                                     lambda eng, kc=kc, j=j, pr=pr, kslot=kslot, il=il, first=first, last=last:
                                                     eng.matmul(ps_o[:, il * 256:(il + 1) * 256],
                                                                vt[:, kslot * 2048 + kc * 128: kslot * 2048 + (kc + 1) * 128],
                                                                pt[:, pr * 1024 + j * 256: pr * 1024 + (j + 1) * 256],
                                                                start=first, stop=last),
                                                     waits=w)
                                            plan.add("tensor",
                                                     lambda eng, j=j, pr=pr, il=il, first=first, last=last:
                                                     eng.matmul(ps_l[:, il * 256:(il + 1) * 256],
                                                                ones128[:],
                                                                pt[:, pr * 1024 + j * 256: pr * 1024 + (j + 1) * 256],
                                                                start=first, stop=last),
                                                     inc=(s_t if j == 3 else None))
                                        pt_done[pr] = s_t.n
                                av_n = s_t.n
                                plan.add("vector",
                                         lambda eng: eng.reciprocal(rl_sb[:], ps_l[:]),
                                         waits=[(s_t, max(av_n, bc_mm))], inc=s_v)
                                rl_n = s_v.n
                                plan.add("tensor",
                                         lambda eng: eng.matmul(ps_bc[:], ones1[:], rl_sb[:],
                                                                start=True, stop=True),
                                         waits=[(s_v, rl_n), (s_sc, bc_read)], inc=s_t)
                                bc_mm = s_t.n
                                plan.add("scalar",
                                         lambda eng: eng.copy(bc_sb[:], ps_bc[:]),
                                         waits=[(s_t, bc_mm), (s_v, head_done_v)], inc=s_sc)
                                bc_read = s_sc.n
                                plan.add("vector",
                                         lambda eng, h=h:
                                         eng.tensor_mul(attn_buf[:, h * 512:(h + 1) * 512],
                                                        ps_o[:], bc_sb[:]),
                                         waits=[(s_sc, bc_read)], inc=s_v)
                                head_done_v = s_v.n
                                qread[hslot] = av_n
                                if L == 1:
                                    est_used = et_free
                                if sync_level >= 3:
                                    barrier()
                            if sync_level >= 2:
                                barrier()
                            stitch_done[kslot] = s_t.n
                        attn_done = head_done_v
                        if sync_level >= 1:
                            barrier()

                    with ExitStack() as stw:
                        ps_w = stw.enter_context(nc.psum_tensor(f"ps_w{L}{H}", [128, 2048], F32))
                        ps_st = stw.enter_context(nc.psum_tensor(f"ps_st{L}{H}", [1, 1024], F32))
                        ps_ab = stw.enter_context(nc.psum_tensor(f"ps_ab{L}{H}", [128, 1024], F32))

                        wo_used = [0, 0]
                        evac_v = 0
                        for ocg in range(8):
                            for jcb in range(8):
                                wslot = (ocg * 8 + jcb) % 2
                                plan.add("gpsimd",
                                         lambda eng, Wd=Wd, ocg=ocg, jcb=jcb, wslot=wslot:
                                         eng.dma_start(out=wo_bf[:, wslot * 2048:(wslot + 1) * 2048],
                                                       in_=Wd["wo"][ocg, jcb, :, :]),
                                         waits=[(s_t, wo_used[wslot])], inc=d_wo[wslot], inc_by=16)
                                gpn = d_wo[wslot].n
                                for jcl in range(4):
                                    for ocl in range(4):
                                        first = (jcb == 0 and jcl == 0)
                                        last = (jcb == 7 and jcl == 3)
                                        w = []
                                        if jcl == 0 and ocl == 0:
                                            w.append((d_wo[wslot], gpn))
                                        if first and ocl == 0:
                                            w.append((s_v, max(attn_done, evac_v)))
                                            if ocg == 0:
                                                w += psum_guard()
                                        plan.add("tensor",
                                                 lambda eng, jcb=jcb, jcl=jcl, ocl=ocl, wslot=wslot, first=first, last=last:
                                                 eng.matmul(ps_w[:, ocl * 512:(ocl + 1) * 512],
                                                            wo_bf[:, wslot * 2048 + (jcl * 4 + ocl) * 128:
                                                                  wslot * 2048 + (jcl * 4 + ocl + 1) * 128],
                                                            attn_buf[:, (jcb * 4 + jcl) * 512:(jcb * 4 + jcl + 1) * 512],
                                                            start=first, stop=last),
                                                 waits=w,
                                                 inc=(s_t if (jcl == 3 and ocl == 3) else None))
                                wo_used[wslot] = s_t.n
                            for ocl in range(4):
                                oc = ocg * 4 + ocl
                                evac_v = emit_epilogue(ps_w, ocl * 512, BO + oc, oc, H,
                                                       ps_st, 0, oc == 0, oc == 31, t_buf)
                        if sync_level >= 1:
                            barrier()
                        emit_ln(H, ps_st, 0, ps_ab, subidx_attn, t_buf)
                        if sync_level >= 1:
                            barrier()
                        plan.flush()

            with ExitStack() as stf0:
                ps_st2 = stf0.enter_context(nc.psum_tensor(f"ps_st2{L}", [1, 2048], F32))
                if L == 0:
                    qt_sb = stf0.enter_context(nc.sbuf_tensor("qt_sb", [128, 2 * 1024], I32))
                    st_sb = stf0.enter_context(nc.sbuf_tensor("st_sb", [128, 2 * 1024], F32))
                    wd_bf = stf0.enter_context(nc.sbuf_tensor("wd_bf", [128, 2 * 256], BF16))
                else:
                    a_bf = stf0.enter_context(nc.sbuf_tensor("a_bf", [128, 2 * 1024], BF16))
                    f_sb = stf0.enter_context(nc.sbuf_tensor("f_sb", [128, 4 * 1024], BF16))
                    with ExitStack() as stfd:
                        ps_fold = stfd.enter_context(nc.psum_tensor("ps_fold", [128, 1024], F32))
                        fold_ev = 0
                        for fc in range(4):
                            for m in range(8):
                                for half in range(2):
                                    w = []
                                    if m == 0 and half == 0:
                                        w = psum_guard() + [(s_v, fold_ev)]
                                        if fc == 0:
                                            w.append((d_ms, d_ms.n))
                                    plan.add("tensor",
                                             lambda eng, fc=fc, m=m, half=half:
                                             eng.matmul(ps_fold[:, half * 512:(half + 1) * 512],
                                                        ident[:],
                                                        x_sb[:, (fc + 4 * m) * 1024 + half * 512:
                                                             (fc + 4 * m) * 1024 + half * 512 + 512],
                                                        start=(m == 0), stop=(m == 7)),
                                             waits=w, inc=(s_t if (m == 7 and half == 1) else None))
                            tn = s_t.n
                            plan.add("vector",
                                     lambda eng, fc=fc:
                                     eng.tensor_copy(f_sb[:, fc * 1024:(fc + 1) * 1024], ps_fold[:]),
                                     waits=[(s_t, tn)], inc=s_v)
                            fold_ev = s_v.n

                with ExitStack() as stf:
                    ps_f = stf.enter_context(nc.psum_tensor(f"ps_f{L}", [128, 2048], F32))
                    evac_f = 0
                    qt_used = [0, 0]
                    wd_used = [0, 0]
                    a_used = [0, 0]
                    for og in range(16):
                        if L == 0:
                            for icg in range(8):
                                islot = icg % 2
                                plan.add("sync",
                                         lambda eng, og=og, icg=icg, islot=islot:
                                         eng.dma_start(out=qt_sb[:, islot * 1024:(islot + 1) * 1024],
                                                       in_=qt0[og, :, icg * 1024:(icg + 1) * 1024]),
                                         waits=[(s_v, qt_used[islot])], inc=d_qt[islot], inc_by=16)
                                plan.add("sync",
                                         lambda eng, og=og, icg=icg, islot=islot:
                                         eng.dma_start(out=st_sb[:, islot * 1024:(islot + 1) * 1024],
                                                       in_=st0[og, :, icg * 1024:(icg + 1) * 1024]),
                                         inc=d_qt[islot], inc_by=16)
                                dman = d_qt[islot].n
                                for icl in range(4):
                                    ic = icg * 4 + icl
                                    icol = islot * 1024 + icl * 256
                                    wds = ic % 2
                                    plan.add("vector",
                                             lambda eng, icol=icol, wds=wds:
                                             eng.scalar_tensor_tensor(wd_bf[:, wds * 256:(wds + 1) * 256],
                                                                      qt_sb[:, icol:icol + 256],
                                                                      -128.0,
                                                                      st_sb[:, icol:icol + 256],
                                                                      ALU.add, ALU.mult),
                                             waits=[(d_qt[islot], dman), (s_t, wd_used[wds])], inc=s_v)
                                    if icl == 3:
                                        qt_used[islot] = s_v.n
                                    wdn = s_v.n
                                    for osc in range(2):
                                        for half in range(2):
                                            w = []
                                            if osc == 0 and half == 0:
                                                w.append((s_v, wdn))
                                                if ic == 0:
                                                    w.append((s_v, evac_f))
                                                    if og == 0:
                                                        w += psum_guard()
                                            plan.add("tensor",
                                                     lambda eng, ic=ic, osc=osc, half=half, wds=wds:
                                                     eng.matmul(ps_f[:, (osc * 2 + half) * 512:(osc * 2 + half + 1) * 512],
                                                                wd_bf[:, wds * 256 + osc * 128: wds * 256 + (osc + 1) * 128],
                                                                x_sb[:, ic * 1024 + half * 512: ic * 1024 + half * 512 + 512],
                                                                start=(ic == 0), stop=(ic == 31)),
                                                     waits=w,
                                                     inc=(s_t if (osc == 1 and half == 1) else None))
                                    wd_used[wds] = s_t.n
                        else:
                            aslot = og % 2
                            plan.add("gpsimd",
                                     lambda eng, og=og, aslot=aslot:
                                     eng.dma_start(out=a_bf[:, aslot * 1024:(aslot + 1) * 1024],
                                                   in_=a1[og, :, :]),
                                     waits=[(s_t, a_used[aslot])], inc=d_a[aslot], inc_by=16)
                            gpn = d_a[aslot].n
                            for fc in range(4):
                                for osc in range(2):
                                    for half in range(2):
                                        w = []
                                        if fc == 0 and osc == 0 and half == 0:
                                            w.append((d_a[aslot], gpn))
                                            w.append((s_v, max(evac_f, fold_ev)))
                                            if og == 0:
                                                w += psum_guard()
                                        plan.add("tensor",
                                                 lambda eng, fc=fc, osc=osc, half=half, aslot=aslot:
                                                 eng.matmul(ps_f[:, (osc * 2 + half) * 512:(osc * 2 + half + 1) * 512],
                                                            a_bf[:, aslot * 1024 + fc * 256 + osc * 128:
                                                                 aslot * 1024 + fc * 256 + (osc + 1) * 128],
                                                            f_sb[:, fc * 1024 + half * 512: fc * 1024 + half * 512 + 512],
                                                            start=(fc == 0), stop=(fc == 3)),
                                                 waits=w,
                                                 inc=(s_t if (fc == 3 and osc == 1 and half == 1) else None))
                            a_used[aslot] = s_t.n
                        for osc in range(2):
                            for half in range(2):
                                oc = og * 2 + osc
                                tsrc = t_buf if half == 0 else attn_buf
                                evac_f = emit_epilogue(ps_f, (osc * 2 + half) * 512, FB + oc, oc, half,
                                                       ps_st2, half * 1024,
                                                       first=(og == 0 and osc == 0),
                                                       last=(og == 15 and osc == 1),
                                                       tsrc=tsrc)
                with ExitStack() as stl:
                    ps_ab2 = stl.enter_context(nc.psum_tensor(f"ps_ab2{L}", [128, 1024], F32))
                    if sync_level >= 1:
                        barrier()
                    emit_ln(0, ps_st2, 0, ps_ab2, subidx_ffn, t_buf)
                    emit_ln(1, ps_st2, 1024, ps_ab2, subidx_ffn, attn_buf)
                    if sync_level >= 1:
                        barrier()
                    plan.flush()

        plan.add("sync", None, waits=[(d_y[0], d_y[0].n), (d_y[1], d_y[1].n)])
        barrier()
        plan.flush()

    return nc


_CACHE = {}


def _lhsT_blocks(Wm):
    I_, F_ = Wm.shape
    return np.ascontiguousarray(
        Wm.reshape(I_ // 128, 128, F_ // 128, 128).transpose(2, 1, 0, 3).reshape(F_ // 128, 128, (I_ // 128) * 128))


def _wo_blocks(Wm):
    r = Wm.reshape(8, 4, 128, 8, 4, 128)
    return np.ascontiguousarray(r.transpose(3, 0, 2, 1, 4, 5).reshape(8, 8, 128, 2048))


def _qt_blocks(qf):
    return np.ascontiguousarray(
        qf.reshape(16, 256, 32, 128).transpose(0, 3, 2, 1).reshape(16, 128, 8192))


def _a_blocks(A):
    return np.ascontiguousarray(
        A.reshape(4, 128, 16, 256).transpose(2, 1, 0, 3).reshape(16, 128, 1024))


def _bias_cols(b):
    return np.ascontiguousarray(b.reshape(-1, 128).T)


def _strips(h):
    p = np.arange(128)[:, None]
    cm = np.arange(MASK_W)[None, :]
    relm = cm - 384 + 256 * h - p
    mask = (relm >= 0).astype(ml_dtypes.bfloat16)

    ce = np.arange(ESTRIP_W)[None, :]
    rele = ce - 384 + 256 * h - p
    base = 2.0 ** (-8.0 / 32.0)
    slopes = (base ** np.arange(1, 33, dtype=np.float32)).astype(np.float32)
    rel_pos = np.maximum(rele, 0).astype(np.float64)
    sq = np.sqrt(rel_pos)
    est = np.empty((NH, 128, ESTRIP_W), dtype=ml_dtypes.bfloat16)
    for hh in range(NH):
        e = np.where(rele >= 0, np.exp(-float(slopes[hh]) * sq), 0.0)
        est[hh] = e.astype(ml_dtypes.bfloat16)
    return mask, est


def _prep_shared(inputs):
    f32 = np.float32
    sh = {}
    sh["wq0"] = _lhsT_blocks(np.asarray(inputs["L0_attn_Wq"], f32))
    sh["wk0"] = _lhsT_blocks(np.asarray(inputs["L0_attn_Wk"], f32))
    sh["wv0"] = np.ascontiguousarray(np.asarray(inputs["L0_attn_Wv"], f32))
    sh["wo0"] = _wo_blocks(np.asarray(inputs["L0_attn_Wo"], f32))
    sh["bq0"] = _bias_cols(np.asarray(inputs["L0_attn_bq"], f32))
    sh["bk0"] = _bias_cols(np.asarray(inputs["L0_attn_bk"], f32))
    sh["bo0"] = _bias_cols(np.asarray(inputs["L0_attn_bo"], f32))
    sh["fb0"] = _bias_cols(np.asarray(inputs["L0_ffn_bias"], f32))
    sh["bv0"] = np.ascontiguousarray(
        np.broadcast_to(np.asarray(inputs["L0_attn_bv"], f32), (128, 1024)))
    qf = np.asarray(inputs["L0_ffn_q_vals"], np.int32).reshape(4096, 4096)
    sh["qt0"] = _qt_blocks(qf)
    sexp = np.repeat(np.asarray(inputs["L0_ffn_q_scales"], f32), 32, axis=1)
    sh["st0"] = _qt_blocks(sexp)
    wqkv = np.asarray(inputs["L1_attn_w_qkv"], f32)
    sh["wq1"] = _lhsT_blocks(np.ascontiguousarray(wqkv[:, :4096]))
    sh["wk1"] = _lhsT_blocks(np.ascontiguousarray(wqkv[:, 4096:5120]))
    sh["wv1"] = np.ascontiguousarray(wqkv[:, 5120:6144])
    sh["wo1"] = _wo_blocks(np.asarray(inputs["L1_attn_w_o"], f32))
    z32 = np.zeros((128, 32), f32)
    z8 = np.zeros((128, 8), f32)
    sh["bq1"] = z32
    sh["bk1"] = z8
    sh["bo1"] = z32
    sh["fb1"] = z32
    sh["bv1"] = np.zeros((128, 1024), f32)
    A = np.asarray(inputs["L1_ffn_kernel"], np.int8).reshape(512, 4096)
    sh["a1"] = _a_blocks(A)
    sh["identc"] = np.eye(128, dtype=ml_dtypes.bfloat16)
    return sh


def _own_tokens(h):
    return np.concatenate([np.arange(512 * i + 256 * h, 512 * i + 256 * h + 256) for i in range(4)])


def kernel(**inputs):
    key = ("nc", TAP, SYNC)
    if key not in _CACHE:
        _CACHE.clear()
        _CACHE[key] = build_program(tap=TAP, sync_level=SYNC)
    nc = _CACHE[key]

    sh = _prep_shared(inputs)
    x_full = np.asarray(inputs["L0_attn_x"], np.float32)

    strips = {h: _strips(h) for h in range(2)}
    in_maps = []
    for c in range(8):
        b, h = c // 2, c % 2
        idx = _own_tokens(h)
        m = dict(sh)
        m["maskc"] = strips[h][0]
        m["estripc"] = strips[h][1]
        m["x0"] = np.ascontiguousarray(x_full[b][idx].T)
        in_maps.append(m)

    res = run_bass_kernel_spmd(nc, in_maps, core_ids=list(range(8)))

    out = np.empty((4, 2048, 4096), np.float32)
    for c in range(8):
        b, h = c // 2, c % 2
        idx = _own_tokens(h)
        out[b, idx, :] = res.results[c]["y"].T
    return out


# version 06
# speedup vs baseline: 1.5915x; 1.5915x over previous
import numpy as np
import ml_dtypes
from contextlib import ExitStack

import concourse.bass as bass
import concourse.mybir as mybir
from concourse.bass_utils import run_bass_kernel_spmd

F32 = mybir.dt.float32
BF16 = mybir.dt.bfloat16
I32 = mybir.dt.int32
I8 = mybir.dt.int8
ALU = mybir.AluOpType
ACT = mybir.ActivationFunctionType

D = 4096
DC = 32
NH = 32
NKV = 8
SCALE = float(1.0 / np.sqrt(np.float32(128.0)))
EXP_OFF = -12.0
EPS = 1e-5
ESTRIP_W = 2176
MASK_W = 640
INV_D = 1.0 / 4096.0

ENGINES = ("sync", "tensor", "vector", "scalar", "gpsimd")

TAP = None
SYNC = 0


class Sem:
    __slots__ = ("h", "n")

    def __init__(self, h):
        self.h = h
        self.n = 0


class Plan:
    def __init__(self, nc):
        self.nc = nc
        self.ops = {e: [] for e in ENGINES}
        self.last_wait = {e: {} for e in ENGINES}

    def add(self, engine, builder, waits=(), inc=None, inc_by=1):
        lw = self.last_wait[engine]
        eff = []
        for (s, v) in waits:
            if v > 0 and lw.get(id(s), -1) < v:
                eff.append((s.h, v))
                lw[id(s)] = v
        if inc is not None:
            inc.n += inc_by

        def fn(eng, builder=builder, w=tuple(eff), inch=(inc.h if inc is not None else None), inc_by=inc_by):
            for (sh, v) in w:
                eng.wait_ge(sh, v)
            if builder is not None:
                ins = builder(eng)
                if inch is not None:
                    ins.then_inc(inch, inc_by)
            elif inch is not None:
                eng.nop().then_inc(inch, inc_by)

        self.ops[engine].append(fn)

    def flush(self):
        if not any(self.ops.values()):
            return
        with self.nc.Block() as block:
            for e in ENGINES:
                lst = self.ops[e]
                if not lst:
                    continue

                def mk(lst):
                    def body(eng):
                        for f in lst:
                            f(eng)
                    return body

                getattr(block, e)(mk(lst))
        self.ops = {e: [] for e in ENGINES}


def build_program(tap=None, sync_level=0):
    nc = bass.Bass("TRN2", target_bir_lowering=False)

    def din(name, shape, dt):
        return nc.dram_tensor(name, list(shape), dt, kind="ExternalInput").ap()

    x0 = din("x0", (D, 1024), F32)
    y = nc.dram_tensor("y", [D, 1024], F32, kind="ExternalOutput").ap()

    W = {}
    for L in (0, 1):
        W[L] = dict(
            wq=din(f"wq{L}", (32, 128, 4096), F32),
            wk=din(f"wk{L}", (8, 128, 4096), F32),
            wv=din(f"wv{L}", (4096, 1024), F32),
            wo=din(f"wo{L}", (8, 8, 128, 2048), F32),
            bq=din(f"bq{L}", (128, 32), F32),
            bk=din(f"bk{L}", (128, 8), F32),
            bo=din(f"bo{L}", (128, 32), F32),
            fb=din(f"fb{L}", (128, 32), F32),
            bv=din(f"bv{L}", (128, 1024), F32),
        )
    qt0 = din("qt0", (16, 128, 8192), I32)
    st0 = din("st0", (16, 128, 8192), F32)
    a1 = din("a1", (16, 128, 1024), I8)
    maskc = din("maskc", (128, MASK_W), BF16)
    estripc = din("estripc", (NH, 128, ESTRIP_W), BF16)
    identc = din("identc", (128, 128), BF16)

    cc_in = {L: nc.dram_tensor(f"cc_in{L}", [2048, 1024], BF16).ap() for L in (0, 1)}
    cc_out = {L: nc.dram_tensor(f"cc_out{L}", [4096, 1024], BF16).ap() for L in (0, 1)}

    with ExitStack() as st_outer:
        def mksem(name):
            return Sem(st_outer.enter_context(nc.semaphore(name)))

        s_t, s_v, s_sc, s_bar = [mksem(n) for n in ("t", "v", "sc", "bar")]
        s_cc = [mksem("cc0"), mksem("cc1")]
        d_ms = mksem("dms")
        d_x0 = mksem("dx0")
        d_bias = mksem("dbias")
        d_es = mksem("des")
        d_wk = [mksem(f"dwk{i}") for i in range(2)]
        d_wv = [mksem(f"dwv{i}") for i in range(2)]
        d_ks = [mksem(f"dks{i}") for i in range(2)]
        d_vs = [mksem(f"dvs{i}") for i in range(2)]
        d_st = [mksem(f"dst{i}") for i in range(2)]
        d_wq = [mksem(f"dwq{i}") for i in range(3)]
        d_wo = [mksem(f"dwo{i}") for i in range(3)]
        d_qt = [mksem(f"dqt{i}") for i in range(3)]
        d_sq = [mksem(f"dsq{i}") for i in range(3)]
        d_a = [mksem(f"da{i}") for i in range(2)]
        d_y = [mksem(f"dy{i}") for i in range(2)]

        def sbo(name, shape, dt):
            return st_outer.enter_context(nc.sbuf_tensor(name, list(shape), dt))

        x_sb = sbo("x_sb", (128, DC * 1024), BF16)
        attn_buf = sbo("attn_buf", (128, 32 * 512), BF16)
        t_buf = sbo("t_buf", (128, 32 * 512), BF16)
        cpool = sbo("cpool", (128, 8), F32)
        ones128 = sbo("ones128", (128, 1), BF16)
        ones1 = sbo("ones1", (1, 128), F32)
        ones1b = sbo("ones1b", (1, 128), BF16)
        ident = sbo("ident", (128, 128), BF16)
        mask_sb = sbo("mask_sb", (128, MASK_W), BF16)
        bias_sb = sbo("bias_sb", (128, 104), F32)
        bv_sb = sbo("bv_sb", (128, 1024), F32)
        tsq_sb = sbo("tsq_sb", (128, 4 * 512), BF16)
        ab_sb = sbo("ab_sb", (128, 1024), F32)
        bc_sb = sbo("bc_sb", (128, 512), F32)
        outst_sb = sbo("outst_sb", (128, 2 * 512), F32)
        ln_sb = sbo("ln_sb", (1, 4 * 512), F32)
        rl_bf = sbo("rl_bf", (1, 512), BF16)

        for i, c in enumerate([0.0, 1.0, EXP_OFF, EPS, -0.5]):
            nc.const_aps.register(F32, c, cpool[:, i:i + 1])

        plan = Plan(nc)
        BQ, BK, BO, FB = 0, 32, 40, 72
        ysent = [0, 0]
        tsq_free = [0, 0, 0, 0]
        bias_cnt = [0]

        def barrier():
            tgt = s_bar.n + 5
            for e in ENGINES:
                plan.add(e, lambda eng: eng.nop(), inc=s_bar)
            assert s_bar.n == tgt
            for e in ENGINES:
                plan.add(e, None, waits=[(s_bar, tgt)])

        def psum_guard():
            return [(s_v, s_v.n), (s_sc, s_sc.n)]

        for i, c in enumerate([0.0, 1.0, EXP_OFF, EPS, -0.5]):
            plan.add("vector", lambda eng, i=i, c=c: eng.memset(cpool[:, i:i + 1], c))
        plan.add("vector", lambda eng: eng.memset(ones128[:], 1.0))
        plan.add("vector", lambda eng: eng.memset(ones1[:], 1.0))
        plan.add("vector", lambda eng: eng.memset(ones1b[:], 1.0))
        plan.add("sync", lambda eng: eng.dma_start(out=ident[:], in_=identc), inc=d_ms, inc_by=16)
        plan.add("sync", lambda eng: eng.dma_start(out=mask_sb[:], in_=maskc), inc=d_ms, inc_by=16)
        for c in range(DC):
            plan.add("gpsimd",
                     lambda eng, c=c: eng.dma_start(out=x_sb[:, c * 1024:(c + 1) * 1024],
                                                    in_=x0[c * 128:(c + 1) * 128, :]),
                     inc=d_x0, inc_by=16)
        barrier()
        plan.flush()

        def emit_ln(half, ps_st, st_off, ps_ab, subidx, tsrc):
            M, E2, MSQ, RSTD = range(4)
            VAR, BP, BP2 = E2, MSQ, MSQ

            def lnc(i):
                return ln_sb[:, i * 512:(i + 1) * 512]

            tn = s_t.n
            plan.add("vector", lambda eng: eng.tensor_scalar_mul(
                lnc(M), ps_st[:, st_off:st_off + 512], INV_D), waits=[(s_t, tn)], inc=s_v)
            plan.add("vector", lambda eng: eng.tensor_scalar_mul(
                lnc(E2), ps_st[:, st_off + 512:st_off + 1024], INV_D), inc=s_v)
            plan.add("vector", lambda eng: eng.tensor_mul(lnc(MSQ), lnc(M), lnc(M)), inc=s_v)
            plan.add("vector", lambda eng: eng.tensor_sub(lnc(VAR), lnc(E2), lnc(MSQ)), inc=s_v)
            vn = s_v.n
            plan.add("scalar", lambda eng: eng.activation(lnc(BP), lnc(VAR), ACT.Ln, bias=EPS, scale=1.0),
                     waits=[(s_v, vn)], inc=s_sc)
            plan.add("scalar", lambda eng: eng.activation(lnc(RSTD), lnc(BP), ACT.Exp, bias=0.0, scale=-0.5),
                     inc=s_sc)
            scn = s_sc.n
            plan.add("vector", lambda eng: eng.tensor_mul(lnc(BP2), lnc(M), lnc(RSTD)),
                     waits=[(s_sc, scn)], inc=s_v)
            vn = s_v.n
            plan.add("tensor", lambda eng: eng.matmul(
                ps_ab[:, 0:512], ones1[:], lnc(RSTD), start=True, stop=True),
                waits=[(s_v, vn)] + psum_guard())
            plan.add("tensor", lambda eng: eng.matmul(
                ps_ab[:, 512:1024], ones1[:], lnc(BP2), start=True, stop=True), inc=s_t)
            tn = s_t.n
            gv = s_v.n
            plan.add("scalar", lambda eng: eng.copy(ab_sb[:, 0:512], ps_ab[:, 0:512]),
                     waits=[(s_t, tn), (s_v, gv)], inc=s_sc)
            plan.add("scalar", lambda eng: eng.copy(ab_sb[:, 512:1024], ps_ab[:, 512:1024]),
                     inc=s_sc)
            scn = s_sc.n

            write_y = (tap == subidx) if tap is not None else (subidx == 3)
            write_x = subidx != 3
            for oc in range(32):
                plan.add("vector", lambda eng, oc=oc:
                         eng.tensor_mul(bc_sb[:], tsrc[:, oc * 512:(oc + 1) * 512],
                                        ab_sb[:, 0:512]),
                         waits=[(s_sc, scn)], inc=s_v)
                if write_x:
                    plan.add("vector", lambda eng, oc=oc, half=half:
                             eng.tensor_sub(x_sb[:, oc * 1024 + half * 512: oc * 1024 + half * 512 + 512],
                                            bc_sb[:], ab_sb[:, 512:1024]),
                             inc=s_v)
                if write_y:
                    oslot = oc % 2
                    w = [(d_y[oslot], ysent[oslot])]
                    plan.add("vector", lambda eng, oslot=oslot:
                             eng.tensor_sub(outst_sb[:, oslot * 512:(oslot + 1) * 512],
                                            bc_sb[:], ab_sb[:, 512:1024]),
                             waits=w, inc=s_v)
                    vn2 = s_v.n
                    plan.add("sync" if oslot == 0 else "gpsimd",
                             lambda eng, oc=oc, oslot=oslot, half=half:
                             eng.dma_start(out=y[oc * 128:(oc + 1) * 128, half * 512:half * 512 + 512],
                                           in_=outst_sb[:, oslot * 512:(oslot + 1) * 512]),
                             waits=[(s_v, vn2)], inc=d_y[oslot], inc_by=16)
                    ysent[oslot] = d_y[oslot].n

        def emit_epilogue(ps_src, ps_sl, bias_col, oc, half, ps_st, st_off, first, last, tsrc, slot):
            tn = s_t.n
            plan.add("vector", lambda eng, ps_src=ps_src, ps_sl=ps_sl, bias_col=bias_col, oc=oc, half=half, tsrc=tsrc:
                     eng.scalar_tensor_tensor(tsrc[:, oc * 512:(oc + 1) * 512],
                                              ps_src[:, ps_sl:ps_sl + 512],
                                              bias_sb[:, bias_col:bias_col + 1],
                                              x_sb[:, oc * 1024 + half * 512: oc * 1024 + half * 512 + 512],
                                              ALU.add, ALU.add),
                     waits=[(s_t, tn), (d_bias, bias_cnt[0])], inc=s_v)
            evac = s_v.n
            plan.add("vector", lambda eng, slot=slot, oc=oc, tsrc=tsrc:
                     eng.tensor_mul(tsq_sb[:, slot * 512:(slot + 1) * 512],
                                    tsrc[:, oc * 512:(oc + 1) * 512],
                                    tsrc[:, oc * 512:(oc + 1) * 512]),
                     waits=[(s_t, tsq_free[slot])], inc=s_v)
            vn = s_v.n

            def rows():
                plan.add("tensor", lambda eng, oc=oc, tsrc=tsrc, ps_st=ps_st, st_off=st_off, first=first, last=last:
                         eng.matmul(ps_st[:, st_off:st_off + 512], ones128[:],
                                    tsrc[:, oc * 512:(oc + 1) * 512], start=first, stop=last),
                         waits=[(s_v, vn)])
                plan.add("tensor", lambda eng, slot=slot, ps_st=ps_st, st_off=st_off, first=first, last=last:
                         eng.matmul(ps_st[:, st_off + 512:st_off + 1024], ones128[:],
                                    tsq_sb[:, slot * 512:(slot + 1) * 512], start=first, stop=last),
                         inc=s_t)
                tsq_free[slot] = s_t.n
            return evac, rows

        for L in (0, 1):
            Wd = W[L]
            subidx_attn = 2 * L
            subidx_ffn = 2 * L + 1

            bw = [(s_v, s_v.n), (s_sc, s_sc.n), (s_t, s_t.n)]
            plan.add("sync", lambda eng, Wd=Wd: eng.dma_start(out=bias_sb[:, 0:32], in_=Wd["bq"]),
                     waits=bw, inc=d_bias, inc_by=16)
            plan.add("sync", lambda eng, Wd=Wd: eng.dma_start(out=bias_sb[:, 32:40], in_=Wd["bk"]),
                     inc=d_bias, inc_by=16)
            plan.add("sync", lambda eng, Wd=Wd: eng.dma_start(out=bias_sb[:, 40:72], in_=Wd["bo"]),
                     inc=d_bias, inc_by=16)
            plan.add("sync", lambda eng, Wd=Wd: eng.dma_start(out=bias_sb[:, 72:104], in_=Wd["fb"]),
                     inc=d_bias, inc_by=16)
            plan.add("sync", lambda eng, Wd=Wd: eng.dma_start(out=bv_sb[:], in_=Wd["bv"]),
                     inc=d_bias, inc_by=16)
            bias_cnt[0] = d_bias.n
            barrier()
            plan.flush()

            with ExitStack() as stp:
                wk_bf = stp.enter_context(nc.sbuf_tensor(f"wk_bf{L}", [128, 2 * 2048], BF16))
                kown = stp.enter_context(nc.sbuf_tensor(f"kown{L}", [128, 2 * 1024], BF16))
                ps_kv = stp.enter_context(nc.psum_tensor(f"ps_kv{L}", [128, 1024], F32))

                wk_used = [0, 0]
                kevac_last = 0
                ksent = [0, 0]
                for kv in range(NKV):
                    for ich in range(2):
                        slot = (kv * 2 + ich) % 2
                        plan.add("gpsimd",
                                 lambda eng, Wd=Wd, kv=kv, ich=ich, slot=slot:
                                 eng.dma_start(out=wk_bf[:, slot * 2048:(slot + 1) * 2048],
                                               in_=Wd["wk"][kv, :, ich * 2048:(ich + 1) * 2048]),
                                 waits=[(s_t, wk_used[slot])], inc=d_wk[slot], inc_by=16)
                        gpn = d_wk[slot].n
                        for icl in range(16):
                            ic = ich * 16 + icl
                            w = [(d_wk[slot], gpn)] if icl == 0 else []
                            if ic == 0:
                                w += psum_guard() if kv == 0 else [(s_v, kevac_last)]
                                if kv == 0:
                                    w.append((d_x0, d_x0.n))
                            plan.add("tensor",
                                     lambda eng, ic=ic, slot=slot, icl=icl:
                                     eng.matmul(ps_kv[:, 0:512],
                                                wk_bf[:, slot * 2048 + icl * 128: slot * 2048 + (icl + 1) * 128],
                                                x_sb[:, ic * 1024: ic * 1024 + 512],
                                                start=(ic == 0), stop=(ic == 31)),
                                     waits=w)
                            plan.add("tensor",
                                     lambda eng, ic=ic, slot=slot, icl=icl:
                                     eng.matmul(ps_kv[:, 512:1024],
                                                wk_bf[:, slot * 2048 + icl * 128: slot * 2048 + (icl + 1) * 128],
                                                x_sb[:, ic * 1024 + 512: ic * 1024 + 1024],
                                                start=(ic == 0), stop=(ic == 31)),
                                     inc=(s_t if icl == 15 else None))
                        wk_used[slot] = s_t.n
                    tn = s_t.n
                    kslot = kv % 2
                    plan.add("vector",
                             lambda eng, kv=kv, kslot=kslot:
                             eng.tensor_scalar_add(kown[:, kslot * 1024:(kslot + 1) * 1024],
                                                   ps_kv[:], bias_sb[:, BK + kv:BK + kv + 1]),
                             waits=[(s_t, tn), (d_ks[kslot], ksent[kslot]),
                                    (d_bias, bias_cnt[0])], inc=s_v)
                    kevac_last = s_v.n
                    vn = s_v.n
                    plan.add("sync",
                             lambda eng, L=L, kv=kv, kslot=kslot:
                             eng.dma_start(out=cc_in[L][kv * 128:(kv + 1) * 128, :],
                                           in_=kown[:, kslot * 1024:(kslot + 1) * 1024]),
                             waits=[(s_v, vn)], inc=d_ks[kslot], inc_by=16)
                    ksent[kslot] = d_ks[kslot].n
                plan.flush()

            with ExitStack() as stp:
                wv_bf = stp.enter_context(nc.sbuf_tensor(f"wv_bf{L}", [128, 2 * 512], BF16))
                vown = stp.enter_context(nc.sbuf_tensor(f"vown{L}", [128, 2 * 512], BF16))
                ps_v = stp.enter_context(nc.psum_tensor(f"ps_v{L}", [128, 4096], F32))

                wvu = [0, 0]
                vsent = [0, 0]
                for fh in range(2):
                    for ic in range(32):
                        slot = ic % 2
                        w = [(s_t, wvu[slot])]
                        if fh == 1 and ic == 0:
                            w.append((s_v, s_v.n))
                        plan.add("gpsimd",
                                 lambda eng, Wd=Wd, fh=fh, ic=ic, slot=slot:
                                 eng.dma_start(out=wv_bf[:, slot * 512:(slot + 1) * 512],
                                               in_=Wd["wv"][ic * 128:(ic + 1) * 128, fh * 512:(fh + 1) * 512]),
                                 waits=w, inc=d_wv[slot], inc_by=16)
                        gpn = d_wv[slot].n
                        for tc in range(8):
                            w = [(d_wv[slot], gpn)] if tc == 0 else []
                            if fh == 0 and ic == 0 and tc == 0:
                                w += psum_guard()
                            plan.add("tensor",
                                     lambda eng, ic=ic, tc=tc, slot=slot:
                                     eng.matmul(ps_v[:, tc * 512:(tc + 1) * 512],
                                                x_sb[:, ic * 1024 + tc * 128: ic * 1024 + (tc + 1) * 128],
                                                wv_bf[:, slot * 512:(slot + 1) * 512],
                                                start=(ic == 0), stop=(ic == 31)),
                                     waits=w, inc=(s_t if tc == 7 else None))
                        wvu[slot] = s_t.n
                    tn = s_t.n
                    for tc in range(8):
                        vslot = tc % 2
                        w = [(s_t, tn), (d_vs[vslot], vsent[vslot]),
                             (d_bias, bias_cnt[0])]
                        plan.add("vector",
                                 lambda eng, fh=fh, tc=tc, vslot=vslot:
                                 eng.tensor_tensor(vown[:, vslot * 512:(vslot + 1) * 512],
                                                   ps_v[:, tc * 512:(tc + 1) * 512],
                                                   bv_sb[:, fh * 512:(fh + 1) * 512], ALU.add),
                                 waits=w, inc=s_v)
                        vn = s_v.n
                        plan.add("sync",
                                 lambda eng, L=L, fh=fh, tc=tc, vslot=vslot:
                                 eng.dma_start(out=cc_in[L][1024 + tc * 128: 1024 + (tc + 1) * 128,
                                                            fh * 512:(fh + 1) * 512],
                                               in_=vown[:, vslot * 512:(vslot + 1) * 512]),
                                 waits=[(s_v, vn)], inc=d_vs[vslot], inc_by=16)
                        vsent[vslot] = d_vs[vslot].n
                ccw = [(d_ks[0], d_ks[0].n), (d_ks[1], d_ks[1].n),
                       (d_vs[0], d_vs[0].n), (d_vs[1], d_vs[1].n)]
                plan.add("gpsimd",
                         lambda eng, L=L:
                         eng.collective_compute("AllGather", ALU.bypass,
                                                replica_groups=[[0, 1], [2, 3], [4, 5], [6, 7]],
                                                ins=[cc_in[L][:, :]], outs=[cc_out[L][:, :]]),
                         waits=ccw, inc=s_cc[L])
                plan.flush()

            with ExitStack() as stp:
                kt = stp.enter_context(nc.sbuf_tensor(f"kt{L}", [128, 2 * 2048], BF16))
                vt = stp.enter_context(nc.sbuf_tensor(f"vt{L}", [128, 2 * 2048], BF16))
                q_bf = stp.enter_context(nc.sbuf_tensor(f"q_bf{L}", [128, 2 * 512], BF16))
                pt = stp.enter_context(nc.sbuf_tensor(f"pt{L}", [128, 3 * 1024], BF16))
                et = stp.enter_context(nc.sbuf_tensor(f"et{L}", [128, 1024], BF16))
                wq_bf = stp.enter_context(nc.sbuf_tensor(f"wq_bf{L}", [128, 3 * 1024], BF16))
                wo_bf = stp.enter_context(nc.sbuf_tensor(f"wo_bf{L}", [128, 3 * 2048], BF16))
                estrip = (stp.enter_context(nc.sbuf_tensor("estrip1", [128, ESTRIP_W], BF16))
                          if L == 1 else None)

                stitch_done = [0, 0]
                wq_used = [0, 0, 0]
                wqctr = [0]
                est_used = 0
                bc_mm = [0]
                hdv = [0]
                av_done = [0]
                qread = [0, 0]
                PTR = 3
                MAXPEND = 2
                for H in range(2):
                    with ExitStack() as stq:
                        ps_q = [stq.enter_context(nc.psum_tensor(f"ps_q{L}{H}{b}", [128, 512], F32))
                                for b in range(2)]
                        ps_s = [stq.enter_context(nc.psum_tensor(f"ps_s{L}{H}{r}", [128, 1024], F32))
                                for r in range(2)]
                        ps_o = stq.enter_context(nc.psum_tensor(f"ps_o{L}{H}", [128, 512], F32))
                        ps_l = stq.enter_context(nc.psum_tensor(f"ps_l{L}{H}", [1, 512], F32))

                        sgrp_exp = [0, 0]
                        psg = [0, 0]
                        pt_done = [0] * PTR
                        et_free = 0
                        ring = 0
                        ptring = 0
                        qev = [0, 0]
                        guard = psum_guard()
                        pend = []

                        def emit_qproj(t):
                            qb = t % 2
                            for ich in range(4):
                                wqs = wqctr[0] % 3
                                wqctr[0] += 1
                                plan.add("gpsimd",
                                         lambda eng, Wd=Wd, t=t, ich=ich, wqs=wqs:
                                         eng.dma_start(out=wq_bf[:, wqs * 1024:(wqs + 1) * 1024],
                                                       in_=Wd["wq"][t, :, ich * 1024:(ich + 1) * 1024]),
                                         waits=[(s_t, wq_used[wqs])], inc=d_wq[wqs], inc_by=16)
                                gpn = d_wq[wqs].n
                                for icl in range(8):
                                    ic = ich * 8 + icl
                                    w = [(d_wq[wqs], gpn)] if icl == 0 else []
                                    if ic == 0:
                                        w += guard
                                        w.append((s_v, qev[qb]))
                                    plan.add("tensor",
                                             lambda eng, ic=ic, wqs=wqs, icl=icl, H=H, qb=qb:
                                             eng.matmul(ps_q[qb][:],
                                                        wq_bf[:, wqs * 1024 + icl * 128: wqs * 1024 + (icl + 1) * 128],
                                                        x_sb[:, ic * 1024 + H * 512: ic * 1024 + H * 512 + 512],
                                                        start=(ic == 0), stop=(ic == 31)),
                                             waits=w, inc=(s_t if icl == 7 else None))
                                wq_used[wqs] = s_t.n
                                if pend:
                                    pend.pop(0)()
                            while pend:
                                pend.pop(0)()
                            qmm_n = s_t.n
                            plan.add("vector",
                                     lambda eng, t=t, qb=qb:
                                     eng.tensor_scalar(q_bf[:, qb * 512:(qb + 1) * 512],
                                                       ps_q[qb][:],
                                                       bias_sb[:, BQ + t:BQ + t + 1],
                                                       SCALE, ALU.add, ALU.mult),
                                     waits=[(s_t, max(qmm_n, qread[qb])),
                                            (d_bias, bias_cnt[0])], inc=s_v)
                            qev[qb] = s_v.n

                        for kv in range(NKV):
                            kslot = kv % 2
                            w = [(s_cc[L], 1), (s_t, stitch_done[kslot])]
                            for s in range(8):
                                plan.add("sync",
                                         lambda eng, L=L, kv=kv, s=s, kslot=kslot:
                                         eng.dma_start(
                                             out=kt[:, kslot * 2048 + s * 256: kslot * 2048 + (s + 1) * 256],
                                             in_=cc_out[L][2048 * (s % 2) + kv * 128:
                                                           2048 * (s % 2) + (kv + 1) * 128,
                                                           256 * (s // 2): 256 * (s // 2) + 256]),
                                         waits=w, inc=d_st[kslot], inc_by=16)
                                w = []
                            for tc in range(16):
                                s = tc // 2
                                ro = 2048 * (s % 2) + 1024 + 256 * (s // 2) + 128 * (tc % 2)
                                plan.add("sync",
                                         lambda eng, L=L, kv=kv, tc=tc, ro=ro, kslot=kslot:
                                         eng.dma_start(
                                             out=vt[:, kslot * 2048 + tc * 128: kslot * 2048 + (tc + 1) * 128],
                                             in_=cc_out[L][ro:ro + 128, kv * 128:(kv + 1) * 128]),
                                         inc=d_st[kslot], inc_by=16)
                            stitch_n = d_st[kslot].n

                            for g in range(4):
                                h = kv * 4 + g
                                hslot = h % 2
                                if L == 1:
                                    plan.add("sync",
                                             lambda eng, h=h:
                                             eng.dma_start(out=estrip[:, :],
                                                           in_=estripc[h, :, :]),
                                             waits=[(s_v, est_used)], inc=d_es, inc_by=16)
                                estrip_n = d_es.n

                                if h == 0:
                                    emit_qproj(0)
                                if h < 31:
                                    emit_qproj(h + 1)

                                for il in range(2):
                                    i = 2 * H + il
                                    for grp in range(i + 1):
                                        r = ring
                                        ring ^= 1
                                        for j in range(4):
                                            kc = 4 * grp + j
                                            w = []
                                            if j == 0:
                                                w = [(s_sc, sgrp_exp[r]),
                                                     (s_v, max(qev[hslot], psg[r])),
                                                     (d_st[kslot], stitch_n)]
                                            plan.add("tensor",
                                                     lambda eng, kc=kc, j=j, r=r, kslot=kslot, hslot=hslot, il=il:
                                                     eng.matmul(ps_s[r][:, j * 256:(j + 1) * 256],
                                                                kt[:, kslot * 2048 + kc * 128: kslot * 2048 + (kc + 1) * 128],
                                                                q_bf[:, hslot * 512 + il * 256: hslot * 512 + (il + 1) * 256],
                                                                start=True, stop=True),
                                                     waits=w, inc=(s_t if j == 3 else None))
                                        smm_n = s_t.n
                                        masked = (L == 1) or (grp == i)
                                        pr = ptring
                                        ptring = (ptring + 1) % PTR
                                        if masked:
                                            plan.add("scalar",
                                                     lambda eng, r=r:
                                                     eng.activation(et[:], ps_s[r][:], ACT.Exp,
                                                                    bias=EXP_OFF, scale=1.0),
                                                     waits=[(s_t, smm_n), (s_v, et_free)], inc=s_sc)
                                            exp_n = s_sc.n
                                            sgrp_exp[r] = exp_n
                                            for j in range(4):
                                                w = [(s_sc, exp_n)]
                                                if j == 0:
                                                    w.append((s_t, pt_done[pr]))
                                                    if L == 1:
                                                        w.append((d_es, estrip_n))
                                                    else:
                                                        w.append((d_ms, d_ms.n))
                                                if L == 0:
                                                    def mkb(j=j, pr=pr):
                                                        def b(eng):
                                                            return eng.tensor_mul(
                                                                pt[:, pr * 1024 + j * 256: pr * 1024 + (j + 1) * 256],
                                                                et[:, j * 256:(j + 1) * 256],
                                                                mask_sb[:, 384 - 128 * j: 640 - 128 * j])
                                                        return b
                                                else:
                                                    m = 512 * (i - grp) - 128 * j + 384

                                                    def mkb(j=j, pr=pr, m=m):
                                                        def b(eng):
                                                            return eng.tensor_mul(
                                                                pt[:, pr * 1024 + j * 256: pr * 1024 + (j + 1) * 256],
                                                                et[:, j * 256:(j + 1) * 256],
                                                                estrip[:, m: m + 256])
                                                        return b
                                                plan.add("vector", mkb(), waits=w,
                                                         inc=(s_v if j == 3 else None))
                                            et_free = s_v.n
                                            pt_ready = (s_v, s_v.n)
                                        else:
                                            plan.add("scalar",
                                                     lambda eng, r=r, pr=pr:
                                                     eng.activation(pt[:, pr * 1024:(pr + 1) * 1024],
                                                                    ps_s[r][:], ACT.Exp,
                                                                    bias=EXP_OFF, scale=1.0),
                                                     waits=[(s_t, max(smm_n, pt_done[pr]))], inc=s_sc)
                                            sgrp_exp[r] = s_sc.n
                                            pt_ready = (s_sc, s_sc.n)

                                        def mkC(pr=pr, kslot=kslot, il=il, grp=grp, i=i, pt_ready=pt_ready):
                                            def C():
                                                for j in range(4):
                                                    kc = 4 * grp + j
                                                    first = (kc == 0)
                                                    last = (kc == 4 * i + 3)
                                                    w = []
                                                    if j == 0:
                                                        w = [pt_ready]
                                                        if first:
                                                            w.append((s_v, hdv[0]))
                                                    plan.add("tensor",
                                                             lambda eng, kc=kc, j=j, pr=pr, kslot=kslot, il=il, first=first, last=last:
                                                             eng.matmul(ps_o[:, il * 256:(il + 1) * 256],
                                                                        vt[:, kslot * 2048 + kc * 128: kslot * 2048 + (kc + 1) * 128],
                                                                        pt[:, pr * 1024 + j * 256: pr * 1024 + (j + 1) * 256],
                                                                        start=first, stop=last),
                                                             waits=w)
                                                    plan.add("tensor",
                                                             lambda eng, j=j, pr=pr, il=il, first=first, last=last:
                                                             eng.matmul(ps_l[:, il * 256:(il + 1) * 256],
                                                                        ones128[:],
                                                                        pt[:, pr * 1024 + j * 256: pr * 1024 + (j + 1) * 256],
                                                                        start=first, stop=last),
                                                             inc=(s_t if j == 3 else None))
                                                pt_done[pr] = s_t.n
                                                av_done[0] = s_t.n
                                            return C
                                        pend.append(mkC())
                                        while len(pend) > MAXPEND:
                                            pend.pop(0)()
                                qread[hslot] = s_t.n
                                if L == 1:
                                    est_used = et_free

                                def mkD(h=h, kslot=kslot, ps_o=ps_o, ps_l=ps_l):
                                    def D():
                                        av_n = av_done[0]
                                        stitch_done[kslot] = av_n
                                        sX = ring ^ 1
                                        plan.add("vector",
                                                 lambda eng: eng.reciprocal(ln_sb[:, 0:512], ps_l[:]),
                                                 waits=[(s_t, max(av_n, bc_mm[0]))], inc=s_v)
                                        plan.add("vector",
                                                 lambda eng: eng.tensor_copy(rl_bf[:], ln_sb[:, 0:512]),
                                                 inc=s_v)
                                        rl_n = s_v.n
                                        plan.add("tensor",
                                                 lambda eng, sX=sX: eng.matmul(ps_s[sX][:, 0:512],
                                                                               ones1b[:], rl_bf[:],
                                                                               start=True, stop=True),
                                                 waits=[(s_v, rl_n), (s_sc, sgrp_exp[sX])], inc=s_t)
                                        bc_mm[0] = s_t.n
                                        plan.add("vector",
                                                 lambda eng, sX=sX: eng.tensor_copy(bc_sb[:], ps_s[sX][:, 0:512]),
                                                 waits=[(s_t, bc_mm[0])], inc=s_v)
                                        plan.add("vector",
                                                 lambda eng, h=h:
                                                 eng.tensor_mul(attn_buf[:, h * 512:(h + 1) * 512],
                                                                ps_o[:], bc_sb[:]),
                                                 inc=s_v)
                                        hdv[0] = s_v.n
                                        psg[sX] = s_v.n
                                    return D
                                pend.append(mkD())
                        for f in pend:
                            f()
                        pend = []
                        attn_done = hdv[0]
                        if sync_level >= 1:
                            barrier()

                    with ExitStack() as stw:
                        ps_w = stw.enter_context(nc.psum_tensor(f"ps_w{L}{H}", [128, 2048], F32))
                        ps_st = stw.enter_context(nc.psum_tensor(f"ps_st{L}{H}", [1, 1024], F32))
                        ps_ab = stw.enter_context(nc.psum_tensor(f"ps_ab{L}{H}", [128, 1024], F32))

                        wo_used = [0, 0, 0]
                        evac_v = 0
                        pend_ep = []
                        for ocg in range(8):
                            for jcb in range(8):
                                wslot = (ocg * 8 + jcb) % 3
                                plan.add("gpsimd",
                                         lambda eng, Wd=Wd, ocg=ocg, jcb=jcb, wslot=wslot:
                                         eng.dma_start(out=wo_bf[:, wslot * 2048:(wslot + 1) * 2048],
                                                       in_=Wd["wo"][ocg, jcb, :, :]),
                                         waits=[(s_t, wo_used[wslot])], inc=d_wo[wslot], inc_by=16)
                                gpn = d_wo[wslot].n
                                for jcl in range(4):
                                    for ocl in range(4):
                                        first = (jcb == 0 and jcl == 0)
                                        last = (jcb == 7 and jcl == 3)
                                        w = []
                                        if jcl == 0 and ocl == 0:
                                            w.append((d_wo[wslot], gpn))
                                        if first and ocl == 0:
                                            w.append((s_v, max(attn_done, evac_v)))
                                            if ocg == 0:
                                                w += psum_guard()
                                        plan.add("tensor",
                                                 lambda eng, jcb=jcb, jcl=jcl, ocl=ocl, wslot=wslot, first=first, last=last:
                                                 eng.matmul(ps_w[:, ocl * 512:(ocl + 1) * 512],
                                                            wo_bf[:, wslot * 2048 + (jcl * 4 + ocl) * 128:
                                                                  wslot * 2048 + (jcl * 4 + ocl + 1) * 128],
                                                            attn_buf[:, (jcb * 4 + jcl) * 512:(jcb * 4 + jcl + 1) * 512],
                                                            start=first, stop=last),
                                                 waits=w,
                                                 inc=(s_t if (jcl == 3 and ocl == 3) else None))
                                wo_used[wslot] = s_t.n
                            for f in pend_ep:
                                f()
                            pend_ep = []
                            for ocl in range(4):
                                oc = ocg * 4 + ocl
                                evac_v, rows = emit_epilogue(ps_w, ocl * 512, BO + oc, oc, H,
                                                             ps_st, 0, oc == 0, oc == 31, t_buf,
                                                             slot=ocl)
                                pend_ep.append(rows)
                        for f in pend_ep:
                            f()
                        pend_ep = []
                        if sync_level >= 1:
                            barrier()
                        emit_ln(H, ps_st, 0, ps_ab, subidx_attn, t_buf)
                        if sync_level >= 1:
                            barrier()
                        plan.flush()

            with ExitStack() as stf0:
                ps_st2 = stf0.enter_context(nc.psum_tensor(f"ps_st2{L}", [1, 2048], F32))
                if L == 0:
                    qt_sb = stf0.enter_context(nc.sbuf_tensor("qt_sb", [128, 3 * 1024], I32))
                    st_sb = stf0.enter_context(nc.sbuf_tensor("st_sb", [128, 3 * 1024], F32))
                    wd_bf = stf0.enter_context(nc.sbuf_tensor("wd_bf", [128, 2 * 256], BF16))
                else:
                    a_bf = stf0.enter_context(nc.sbuf_tensor("a_bf", [128, 2 * 1024], BF16))
                    f_sb = stf0.enter_context(nc.sbuf_tensor("f_sb", [128, 4 * 1024], BF16))
                    with ExitStack() as stfd:
                        ps_fold = stfd.enter_context(nc.psum_tensor("ps_fold", [128, 1024], F32))
                        fold_ev = 0
                        for fc in range(4):
                            for m in range(8):
                                for half in range(2):
                                    w = []
                                    if m == 0 and half == 0:
                                        w = psum_guard() + [(s_v, fold_ev)]
                                        if fc == 0:
                                            w.append((d_ms, d_ms.n))
                                    plan.add("tensor",
                                             lambda eng, fc=fc, m=m, half=half:
                                             eng.matmul(ps_fold[:, half * 512:(half + 1) * 512],
                                                        ident[:],
                                                        x_sb[:, (fc + 4 * m) * 1024 + half * 512:
                                                             (fc + 4 * m) * 1024 + half * 512 + 512],
                                                        start=(m == 0), stop=(m == 7)),
                                             waits=w, inc=(s_t if (m == 7 and half == 1) else None))
                            tn = s_t.n
                            plan.add("vector",
                                     lambda eng, fc=fc:
                                     eng.tensor_copy(f_sb[:, fc * 1024:(fc + 1) * 1024], ps_fold[:]),
                                     waits=[(s_t, tn)], inc=s_v)
                            fold_ev = s_v.n

                with ExitStack() as stf:
                    ps_f = stf.enter_context(nc.psum_tensor(f"ps_f{L}", [128, 2048], F32))
                    evac_f = 0
                    qt_used = [0, 0, 0]
                    wd_used = [0, 0]
                    a_used = [0, 0]
                    pend_epf = []
                    for og in range(16):
                        if L == 0:
                            for icg in range(8):
                                islot = (og * 8 + icg) % 3
                                plan.add("sync",
                                         lambda eng, og=og, icg=icg, islot=islot:
                                         eng.dma_start(out=qt_sb[:, islot * 1024:(islot + 1) * 1024],
                                                       in_=qt0[og, :, icg * 1024:(icg + 1) * 1024]),
                                         waits=[(s_v, qt_used[islot])], inc=d_qt[islot], inc_by=16)
                                plan.add("gpsimd",
                                         lambda eng, og=og, icg=icg, islot=islot:
                                         eng.dma_start(out=st_sb[:, islot * 1024:(islot + 1) * 1024],
                                                       in_=st0[og, :, icg * 1024:(icg + 1) * 1024]),
                                         waits=[(s_v, qt_used[islot])], inc=d_sq[islot], inc_by=16)
                                dman = d_qt[islot].n
                                sqn = d_sq[islot].n
                                for icl in range(4):
                                    ic = icg * 4 + icl
                                    icol = islot * 1024 + icl * 256
                                    wds = ic % 2
                                    plan.add("vector",
                                             lambda eng, icol=icol, wds=wds:
                                             eng.scalar_tensor_tensor(wd_bf[:, wds * 256:(wds + 1) * 256],
                                                                      qt_sb[:, icol:icol + 256],
                                                                      -128.0,
                                                                      st_sb[:, icol:icol + 256],
                                                                      ALU.add, ALU.mult),
                                             waits=[(d_qt[islot], dman), (d_sq[islot], sqn),
                                                    (s_t, wd_used[wds])], inc=s_v)
                                    if icl == 3:
                                        qt_used[islot] = s_v.n
                                    wdn = s_v.n
                                    for osc in range(2):
                                        for half in range(2):
                                            w = []
                                            if osc == 0 and half == 0:
                                                w.append((s_v, wdn))
                                                if ic == 0:
                                                    w.append((s_v, evac_f))
                                                    if og == 0:
                                                        w += psum_guard()
                                            plan.add("tensor",
                                                     lambda eng, ic=ic, osc=osc, half=half, wds=wds:
                                                     eng.matmul(ps_f[:, (osc * 2 + half) * 512:(osc * 2 + half + 1) * 512],
                                                                wd_bf[:, wds * 256 + osc * 128: wds * 256 + (osc + 1) * 128],
                                                                x_sb[:, ic * 1024 + half * 512: ic * 1024 + half * 512 + 512],
                                                                start=(ic == 0), stop=(ic == 31)),
                                                     waits=w,
                                                     inc=(s_t if (osc == 1 and half == 1) else None))
                                    wd_used[wds] = s_t.n
                        else:
                            aslot = og % 2
                            plan.add("gpsimd",
                                     lambda eng, og=og, aslot=aslot:
                                     eng.dma_start(out=a_bf[:, aslot * 1024:(aslot + 1) * 1024],
                                                   in_=a1[og, :, :]),
                                     waits=[(s_t, a_used[aslot])], inc=d_a[aslot], inc_by=16)
                            gpn = d_a[aslot].n
                            for fc in range(4):
                                for osc in range(2):
                                    for half in range(2):
                                        w = []
                                        if fc == 0 and osc == 0 and half == 0:
                                            w.append((d_a[aslot], gpn))
                                            w.append((s_v, max(evac_f, fold_ev)))
                                            if og == 0:
                                                w += psum_guard()
                                        plan.add("tensor",
                                                 lambda eng, fc=fc, osc=osc, half=half, aslot=aslot:
                                                 eng.matmul(ps_f[:, (osc * 2 + half) * 512:(osc * 2 + half + 1) * 512],
                                                            a_bf[:, aslot * 1024 + fc * 256 + osc * 128:
                                                                 aslot * 1024 + fc * 256 + (osc + 1) * 128],
                                                            f_sb[:, fc * 1024 + half * 512: fc * 1024 + half * 512 + 512],
                                                            start=(fc == 0), stop=(fc == 3)),
                                                 waits=w,
                                                 inc=(s_t if (fc == 3 and osc == 1 and half == 1) else None))
                            a_used[aslot] = s_t.n
                        for f in pend_epf:
                            f()
                        pend_epf = []
                        for osc in range(2):
                            for half in range(2):
                                oc = og * 2 + osc
                                tsrc = t_buf if half == 0 else attn_buf
                                evac_f, rows = emit_epilogue(ps_f, (osc * 2 + half) * 512, FB + oc, oc, half,
                                                             ps_st2, half * 1024,
                                                             first=(og == 0 and osc == 0),
                                                             last=(og == 15 and osc == 1),
                                                             tsrc=tsrc, slot=osc * 2 + half)
                                pend_epf.append(rows)
                    for f in pend_epf:
                        f()
                    pend_epf = []
                with ExitStack() as stl:
                    ps_ab2 = stl.enter_context(nc.psum_tensor(f"ps_ab2{L}", [128, 1024], F32))
                    if sync_level >= 1:
                        barrier()
                    emit_ln(0, ps_st2, 0, ps_ab2, subidx_ffn, t_buf)
                    emit_ln(1, ps_st2, 1024, ps_ab2, subidx_ffn, attn_buf)
                    if sync_level >= 1:
                        barrier()
                    plan.flush()

        plan.add("sync", None, waits=[(d_y[0], d_y[0].n), (d_y[1], d_y[1].n)])
        barrier()
        plan.flush()

    return nc


_CACHE = {}


def _lhsT_blocks(Wm):
    I_, F_ = Wm.shape
    return np.ascontiguousarray(
        Wm.reshape(I_ // 128, 128, F_ // 128, 128).transpose(2, 1, 0, 3).reshape(F_ // 128, 128, (I_ // 128) * 128))


def _wo_blocks(Wm):
    r = Wm.reshape(8, 4, 128, 8, 4, 128)
    return np.ascontiguousarray(r.transpose(3, 0, 2, 1, 4, 5).reshape(8, 8, 128, 2048))


def _qt_blocks(qf):
    return np.ascontiguousarray(
        qf.reshape(16, 256, 32, 128).transpose(0, 3, 2, 1).reshape(16, 128, 8192))


def _a_blocks(A):
    return np.ascontiguousarray(
        A.reshape(4, 128, 16, 256).transpose(2, 1, 0, 3).reshape(16, 128, 1024))


def _bias_cols(b):
    return np.ascontiguousarray(b.reshape(-1, 128).T)


def _strips(h):
    p = np.arange(128)[:, None]
    cm = np.arange(MASK_W)[None, :]
    relm = cm - 384 + 256 * h - p
    mask = (relm >= 0).astype(ml_dtypes.bfloat16)

    ce = np.arange(ESTRIP_W)[None, :]
    rele = ce - 384 + 256 * h - p
    base = 2.0 ** (-8.0 / 32.0)
    slopes = (base ** np.arange(1, 33, dtype=np.float32)).astype(np.float32)
    rel_pos = np.maximum(rele, 0).astype(np.float64)
    sq = np.sqrt(rel_pos)
    est = np.empty((NH, 128, ESTRIP_W), dtype=ml_dtypes.bfloat16)
    for hh in range(NH):
        e = np.where(rele >= 0, np.exp(-float(slopes[hh]) * sq), 0.0)
        est[hh] = e.astype(ml_dtypes.bfloat16)
    return mask, est


def _prep_shared(inputs):
    f32 = np.float32
    sh = {}
    sh["wq0"] = _lhsT_blocks(np.asarray(inputs["L0_attn_Wq"], f32))
    sh["wk0"] = _lhsT_blocks(np.asarray(inputs["L0_attn_Wk"], f32))
    sh["wv0"] = np.ascontiguousarray(np.asarray(inputs["L0_attn_Wv"], f32))
    sh["wo0"] = _wo_blocks(np.asarray(inputs["L0_attn_Wo"], f32))
    sh["bq0"] = _bias_cols(np.asarray(inputs["L0_attn_bq"], f32))
    sh["bk0"] = _bias_cols(np.asarray(inputs["L0_attn_bk"], f32))
    sh["bo0"] = _bias_cols(np.asarray(inputs["L0_attn_bo"], f32))
    sh["fb0"] = _bias_cols(np.asarray(inputs["L0_ffn_bias"], f32))
    sh["bv0"] = np.ascontiguousarray(
        np.broadcast_to(np.asarray(inputs["L0_attn_bv"], f32), (128, 1024)))
    qf = np.asarray(inputs["L0_ffn_q_vals"], np.int32).reshape(4096, 4096)
    sh["qt0"] = _qt_blocks(qf)
    sexp = np.repeat(np.asarray(inputs["L0_ffn_q_scales"], f32), 32, axis=1)
    sh["st0"] = _qt_blocks(sexp)
    wqkv = np.asarray(inputs["L1_attn_w_qkv"], f32)
    sh["wq1"] = _lhsT_blocks(np.ascontiguousarray(wqkv[:, :4096]))
    sh["wk1"] = _lhsT_blocks(np.ascontiguousarray(wqkv[:, 4096:5120]))
    sh["wv1"] = np.ascontiguousarray(wqkv[:, 5120:6144])
    sh["wo1"] = _wo_blocks(np.asarray(inputs["L1_attn_w_o"], f32))
    z32 = np.zeros((128, 32), f32)
    z8 = np.zeros((128, 8), f32)
    sh["bq1"] = z32
    sh["bk1"] = z8
    sh["bo1"] = z32
    sh["fb1"] = z32
    sh["bv1"] = np.zeros((128, 1024), f32)
    A = np.asarray(inputs["L1_ffn_kernel"], np.int8).reshape(512, 4096)
    sh["a1"] = _a_blocks(A)
    sh["identc"] = np.eye(128, dtype=ml_dtypes.bfloat16)
    return sh


def _own_tokens(h):
    return np.concatenate([np.arange(512 * i + 256 * h, 512 * i + 256 * h + 256) for i in range(4)])


def kernel(**inputs):
    key = ("nc", TAP, SYNC)
    if key not in _CACHE:
        _CACHE.clear()
        _CACHE[key] = build_program(tap=TAP, sync_level=SYNC)
    nc = _CACHE[key]

    sh = _prep_shared(inputs)
    x_full = np.asarray(inputs["L0_attn_x"], np.float32)

    strips = {h: _strips(h) for h in range(2)}
    in_maps = []
    for c in range(8):
        b, h = c // 2, c % 2
        idx = _own_tokens(h)
        m = dict(sh)
        m["maskc"] = strips[h][0]
        m["estripc"] = strips[h][1]
        m["x0"] = np.ascontiguousarray(x_full[b][idx].T)
        in_maps.append(m)

    res = run_bass_kernel_spmd(nc, in_maps, core_ids=list(range(8)))

    out = np.empty((4, 2048, 4096), np.float32)
    for c in range(8):
        b, h = c // 2, c % 2
        idx = _own_tokens(h)
        out[b, idx, :] = res.results[c]["y"].T
    return out


# version 07 (final)
# speedup vs baseline: 1.7217x; 1.0818x over previous
import numpy as np
import ml_dtypes
from contextlib import ExitStack

import concourse.bass as bass
import concourse.mybir as mybir
from concourse.bass_utils import run_bass_kernel_spmd

F32 = mybir.dt.float32
BF16 = mybir.dt.bfloat16
I32 = mybir.dt.int32
I8 = mybir.dt.int8
ALU = mybir.AluOpType
ACT = mybir.ActivationFunctionType

D = 4096
DC = 32
NH = 32
NKV = 8
SCALE = float(1.0 / np.sqrt(np.float32(128.0)))
EXP_OFF = -12.0
EPS = 1e-5
ESTRIP_W = 2176
MASK_W = 640
INV_D = 1.0 / 4096.0

ENGINES = ("sync", "tensor", "vector", "scalar", "gpsimd")

TAP = None
SYNC = 0


class Sem:
    __slots__ = ("h", "n")

    def __init__(self, h):
        self.h = h
        self.n = 0


class Plan:
    def __init__(self, nc):
        self.nc = nc
        self.ops = {e: [] for e in ENGINES}
        self.last_wait = {e: {} for e in ENGINES}

    def add(self, engine, builder, waits=(), inc=None, inc_by=1):
        lw = self.last_wait[engine]
        eff = []
        for (s, v) in waits:
            if v > 0 and lw.get(id(s), -1) < v:
                eff.append((s.h, v))
                lw[id(s)] = v
        if inc is not None:
            inc.n += inc_by

        def fn(eng, builder=builder, w=tuple(eff), inch=(inc.h if inc is not None else None), inc_by=inc_by):
            for (sh, v) in w:
                eng.wait_ge(sh, v)
            if builder is not None:
                ins = builder(eng)
                if inch is not None:
                    ins.then_inc(inch, inc_by)
            elif inch is not None:
                eng.nop().then_inc(inch, inc_by)

        self.ops[engine].append(fn)

    def flush(self):
        if not any(self.ops.values()):
            return
        with self.nc.Block() as block:
            for e in ENGINES:
                lst = self.ops[e]
                if not lst:
                    continue

                def mk(lst):
                    def body(eng):
                        for f in lst:
                            f(eng)
                    return body

                getattr(block, e)(mk(lst))
        self.ops = {e: [] for e in ENGINES}


def build_program(tap=None, sync_level=0):
    nc = bass.Bass("TRN2", target_bir_lowering=False)

    def din(name, shape, dt):
        return nc.dram_tensor(name, list(shape), dt, kind="ExternalInput").ap()

    x0 = din("x0", (D, 1024), F32)
    y = nc.dram_tensor("y", [D, 1024], F32, kind="ExternalOutput").ap()

    W = {}
    for L in (0, 1):
        W[L] = dict(
            wq=din(f"wq{L}", (32, 128, 4096), F32),
            wk=din(f"wk{L}", (8, 128, 4096), F32),
            wv=din(f"wv{L}", (4096, 1024), F32),
            wo=din(f"wo{L}", (8, 8, 128, 2048), F32),
            bq=din(f"bq{L}", (128, 32), F32),
            bk=din(f"bk{L}", (128, 8), F32),
            bo=din(f"bo{L}", (128, 32), F32),
            fb=din(f"fb{L}", (128, 32), F32),
            bv=din(f"bv{L}", (128, 1024), F32),
        )
    qt0 = din("qt0", (16, 128, 8192), I32)
    st0 = din("st0", (16, 128, 8192), F32)
    a1 = din("a1", (16, 128, 1024), I8)
    maskc = din("maskc", (128, MASK_W), BF16)
    estripc = din("estripc", (NH, 128, ESTRIP_W), BF16)
    identc = din("identc", (128, 128), BF16)

    cc_in = {L: nc.dram_tensor(f"cc_in{L}", [2048, 1024], BF16).ap() for L in (0, 1)}
    cc_out = {L: nc.dram_tensor(f"cc_out{L}", [4096, 1024], BF16).ap() for L in (0, 1)}

    with ExitStack() as st_outer:
        def mksem(name):
            return Sem(st_outer.enter_context(nc.semaphore(name)))

        s_t, s_v, s_sc, s_bar = [mksem(n) for n in ("t", "v", "sc", "bar")]
        s_cc = [mksem("cc0"), mksem("cc1")]
        d_ms = mksem("dms")
        d_x0 = mksem("dx0")
        d_bias = mksem("dbias")
        d_es = mksem("des")
        d_wk = [mksem(f"dwk{i}") for i in range(2)]
        d_wv = [mksem(f"dwv{i}") for i in range(3)]
        d_ks = [mksem(f"dks{i}") for i in range(2)]
        d_vs = [mksem(f"dvs{i}") for i in range(2)]
        d_st = [mksem(f"dst{i}") for i in range(2)]
        d_wq = [mksem(f"dwq{i}") for i in range(3)]
        d_wo = [mksem(f"dwo{i}") for i in range(3)]
        d_qt = [mksem(f"dqt{i}") for i in range(3)]
        d_sq = [mksem(f"dsq{i}") for i in range(3)]
        d_a = [mksem(f"da{i}") for i in range(2)]
        d_y = [mksem(f"dy{i}") for i in range(2)]

        def sbo(name, shape, dt):
            return st_outer.enter_context(nc.sbuf_tensor(name, list(shape), dt))

        x_sb = sbo("x_sb", (128, DC * 1024), BF16)
        attn_buf = sbo("attn_buf", (128, 32 * 512), BF16)
        t_buf = sbo("t_buf", (128, 32 * 512), BF16)
        cpool = sbo("cpool", (128, 8), F32)
        ones128 = sbo("ones128", (128, 1), BF16)
        ones1 = sbo("ones1", (1, 128), F32)
        ones1b = sbo("ones1b", (1, 128), BF16)
        ident = sbo("ident", (128, 128), BF16)
        mask_sb = sbo("mask_sb", (128, MASK_W), BF16)
        bias_sb = sbo("bias_sb", (128, 104), F32)
        bv_sb = sbo("bv_sb", (128, 1024), F32)
        tsq_sb = sbo("tsq_sb", (128, 4 * 512), BF16)
        ab_sb = sbo("ab_sb", (128, 1024), F32)
        bc_sb = sbo("bc_sb", (128, 512), F32)
        outst_sb = sbo("outst_sb", (128, 2 * 512), F32)
        ln_sb = sbo("ln_sb", (1, 4 * 512), F32)
        rl_bf = sbo("rl_bf", (1, 512), BF16)

        for i, c in enumerate([0.0, 1.0, EXP_OFF, EPS, -0.5]):
            nc.const_aps.register(F32, c, cpool[:, i:i + 1])

        plan = Plan(nc)
        BQ, BK, BO, FB = 0, 32, 40, 72
        ysent = [0, 0]
        tsq_free = [0, 0, 0, 0]
        bias_cnt = [0]

        def barrier():
            tgt = s_bar.n + 5
            for e in ENGINES:
                plan.add(e, lambda eng: eng.nop(), inc=s_bar)
            assert s_bar.n == tgt
            for e in ENGINES:
                plan.add(e, None, waits=[(s_bar, tgt)])

        def psum_guard():
            return [(s_v, s_v.n), (s_sc, s_sc.n)]

        for i, c in enumerate([0.0, 1.0, EXP_OFF, EPS, -0.5]):
            plan.add("vector", lambda eng, i=i, c=c: eng.memset(cpool[:, i:i + 1], c))
        plan.add("vector", lambda eng: eng.memset(ones128[:], 1.0))
        plan.add("vector", lambda eng: eng.memset(ones1[:], 1.0))
        plan.add("vector", lambda eng: eng.memset(ones1b[:], 1.0))
        plan.add("sync", lambda eng: eng.dma_start(out=ident[:], in_=identc), inc=d_ms, inc_by=16)
        plan.add("sync", lambda eng: eng.dma_start(out=mask_sb[:], in_=maskc), inc=d_ms, inc_by=16)
        for c in range(DC):
            plan.add("gpsimd",
                     lambda eng, c=c: eng.dma_start(out=x_sb[:, c * 1024:(c + 1) * 1024],
                                                    in_=x0[c * 128:(c + 1) * 128, :]),
                     inc=d_x0, inc_by=16)
        barrier()
        plan.flush()

        def emit_ln(half, ps_st, st_off, ps_ab, subidx, tsrc):
            M, E2, MSQ, RSTD = range(4)
            VAR, BP, BP2 = E2, MSQ, MSQ

            def lnc(i):
                return ln_sb[:, i * 512:(i + 1) * 512]

            tn = s_t.n
            plan.add("vector", lambda eng: eng.tensor_scalar_mul(
                lnc(M), ps_st[:, st_off:st_off + 512], INV_D), waits=[(s_t, tn)], inc=s_v)
            plan.add("vector", lambda eng: eng.tensor_scalar_mul(
                lnc(E2), ps_st[:, st_off + 512:st_off + 1024], INV_D), inc=s_v)
            plan.add("vector", lambda eng: eng.tensor_mul(lnc(MSQ), lnc(M), lnc(M)), inc=s_v)
            plan.add("vector", lambda eng: eng.tensor_sub(lnc(VAR), lnc(E2), lnc(MSQ)), inc=s_v)
            vn = s_v.n
            plan.add("scalar", lambda eng: eng.activation(lnc(BP), lnc(VAR), ACT.Ln, bias=EPS, scale=1.0),
                     waits=[(s_v, vn)], inc=s_sc)
            plan.add("scalar", lambda eng: eng.activation(lnc(RSTD), lnc(BP), ACT.Exp, bias=0.0, scale=-0.5),
                     inc=s_sc)
            scn = s_sc.n
            plan.add("vector", lambda eng: eng.tensor_mul(lnc(BP2), lnc(M), lnc(RSTD)),
                     waits=[(s_sc, scn)], inc=s_v)
            vn = s_v.n
            plan.add("tensor", lambda eng: eng.matmul(
                ps_ab[:, 0:512], ones1[:], lnc(RSTD), start=True, stop=True),
                waits=[(s_v, vn)] + psum_guard())
            plan.add("tensor", lambda eng: eng.matmul(
                ps_ab[:, 512:1024], ones1[:], lnc(BP2), start=True, stop=True), inc=s_t)
            tn = s_t.n
            gv = s_v.n
            plan.add("scalar", lambda eng: eng.copy(ab_sb[:, 0:512], ps_ab[:, 0:512]),
                     waits=[(s_t, tn), (s_v, gv)], inc=s_sc)
            plan.add("scalar", lambda eng: eng.copy(ab_sb[:, 512:1024], ps_ab[:, 512:1024]),
                     inc=s_sc)
            scn = s_sc.n

            write_y = (tap == subidx) if tap is not None else (subidx == 3)
            write_x = subidx != 3
            for oc in range(32):
                plan.add("vector", lambda eng, oc=oc:
                         eng.tensor_mul(bc_sb[:], tsrc[:, oc * 512:(oc + 1) * 512],
                                        ab_sb[:, 0:512]),
                         waits=[(s_sc, scn)], inc=s_v)
                if write_x:
                    plan.add("vector", lambda eng, oc=oc, half=half:
                             eng.tensor_sub(x_sb[:, oc * 1024 + half * 512: oc * 1024 + half * 512 + 512],
                                            bc_sb[:], ab_sb[:, 512:1024]),
                             inc=s_v)
                if write_y:
                    oslot = oc % 2
                    w = [(d_y[oslot], ysent[oslot])]
                    plan.add("vector", lambda eng, oslot=oslot:
                             eng.tensor_sub(outst_sb[:, oslot * 512:(oslot + 1) * 512],
                                            bc_sb[:], ab_sb[:, 512:1024]),
                             waits=w, inc=s_v)
                    vn2 = s_v.n
                    plan.add("sync" if oslot == 0 else "gpsimd",
                             lambda eng, oc=oc, oslot=oslot, half=half:
                             eng.dma_start(out=y[oc * 128:(oc + 1) * 128, half * 512:half * 512 + 512],
                                           in_=outst_sb[:, oslot * 512:(oslot + 1) * 512]),
                             waits=[(s_v, vn2)], inc=d_y[oslot], inc_by=16)
                    ysent[oslot] = d_y[oslot].n

        def emit_epilogue(ps_src, ps_sl, bias_col, oc, half, ps_st, st_off, first, last, tsrc, slot):
            tn = s_t.n
            plan.add("vector", lambda eng, ps_src=ps_src, ps_sl=ps_sl, bias_col=bias_col, oc=oc, half=half, tsrc=tsrc:
                     eng.scalar_tensor_tensor(tsrc[:, oc * 512:(oc + 1) * 512],
                                              ps_src[:, ps_sl:ps_sl + 512],
                                              bias_sb[:, bias_col:bias_col + 1],
                                              x_sb[:, oc * 1024 + half * 512: oc * 1024 + half * 512 + 512],
                                              ALU.add, ALU.add),
                     waits=[(s_t, tn), (d_bias, bias_cnt[0])], inc=s_v)
            evac = s_v.n
            plan.add("vector", lambda eng, slot=slot, oc=oc, tsrc=tsrc:
                     eng.tensor_mul(tsq_sb[:, slot * 512:(slot + 1) * 512],
                                    tsrc[:, oc * 512:(oc + 1) * 512],
                                    tsrc[:, oc * 512:(oc + 1) * 512]),
                     waits=[(s_t, tsq_free[slot])], inc=s_v)
            vn = s_v.n

            def rows():
                plan.add("tensor", lambda eng, oc=oc, tsrc=tsrc, ps_st=ps_st, st_off=st_off, first=first, last=last:
                         eng.matmul(ps_st[:, st_off:st_off + 512], ones128[:],
                                    tsrc[:, oc * 512:(oc + 1) * 512], start=first, stop=last),
                         waits=[(s_v, vn)])
                plan.add("tensor", lambda eng, slot=slot, ps_st=ps_st, st_off=st_off, first=first, last=last:
                         eng.matmul(ps_st[:, st_off + 512:st_off + 1024], ones128[:],
                                    tsq_sb[:, slot * 512:(slot + 1) * 512], start=first, stop=last),
                         inc=s_t)
                tsq_free[slot] = s_t.n
            return evac, rows

        for L in (0, 1):
            Wd = W[L]
            subidx_attn = 2 * L
            subidx_ffn = 2 * L + 1

            bw = [(s_v, s_v.n), (s_sc, s_sc.n), (s_t, s_t.n)]
            plan.add("sync", lambda eng, Wd=Wd: eng.dma_start(out=bias_sb[:, 0:32], in_=Wd["bq"]),
                     waits=bw, inc=d_bias, inc_by=16)
            plan.add("sync", lambda eng, Wd=Wd: eng.dma_start(out=bias_sb[:, 32:40], in_=Wd["bk"]),
                     inc=d_bias, inc_by=16)
            plan.add("sync", lambda eng, Wd=Wd: eng.dma_start(out=bias_sb[:, 40:72], in_=Wd["bo"]),
                     inc=d_bias, inc_by=16)
            plan.add("sync", lambda eng, Wd=Wd: eng.dma_start(out=bias_sb[:, 72:104], in_=Wd["fb"]),
                     inc=d_bias, inc_by=16)
            plan.add("sync", lambda eng, Wd=Wd: eng.dma_start(out=bv_sb[:], in_=Wd["bv"]),
                     inc=d_bias, inc_by=16)
            bias_cnt[0] = d_bias.n
            barrier()
            plan.flush()

            with ExitStack() as stp:
                wk_bf = stp.enter_context(nc.sbuf_tensor(f"wk_bf{L}", [128, 2 * 2048], BF16))
                kown = stp.enter_context(nc.sbuf_tensor(f"kown{L}", [128, 2 * 1024], BF16))
                ps_kv = stp.enter_context(nc.psum_tensor(f"ps_kv{L}", [128, 1024], F32))

                wk_used = [0, 0]
                kevac_last = 0
                ksent = [0, 0]
                for kv in range(NKV):
                    for ich in range(2):
                        slot = (kv * 2 + ich) % 2
                        plan.add("gpsimd",
                                 lambda eng, Wd=Wd, kv=kv, ich=ich, slot=slot:
                                 eng.dma_start(out=wk_bf[:, slot * 2048:(slot + 1) * 2048],
                                               in_=Wd["wk"][kv, :, ich * 2048:(ich + 1) * 2048]),
                                 waits=[(s_t, wk_used[slot])], inc=d_wk[slot], inc_by=16)
                        gpn = d_wk[slot].n
                        for icl in range(16):
                            ic = ich * 16 + icl
                            w = [(d_wk[slot], gpn)] if icl == 0 else []
                            if ic == 0:
                                w += psum_guard() if kv == 0 else [(s_v, kevac_last)]
                                if kv == 0:
                                    w.append((d_x0, d_x0.n))
                            plan.add("tensor",
                                     lambda eng, ic=ic, slot=slot, icl=icl:
                                     eng.matmul(ps_kv[:, 0:512],
                                                wk_bf[:, slot * 2048 + icl * 128: slot * 2048 + (icl + 1) * 128],
                                                x_sb[:, ic * 1024: ic * 1024 + 512],
                                                start=(ic == 0), stop=(ic == 31)),
                                     waits=w)
                            plan.add("tensor",
                                     lambda eng, ic=ic, slot=slot, icl=icl:
                                     eng.matmul(ps_kv[:, 512:1024],
                                                wk_bf[:, slot * 2048 + icl * 128: slot * 2048 + (icl + 1) * 128],
                                                x_sb[:, ic * 1024 + 512: ic * 1024 + 1024],
                                                start=(ic == 0), stop=(ic == 31)),
                                     inc=(s_t if icl == 15 else None))
                        wk_used[slot] = s_t.n
                    tn = s_t.n
                    kslot = kv % 2
                    plan.add("vector",
                             lambda eng, kv=kv, kslot=kslot:
                             eng.tensor_scalar_add(kown[:, kslot * 1024:(kslot + 1) * 1024],
                                                   ps_kv[:], bias_sb[:, BK + kv:BK + kv + 1]),
                             waits=[(s_t, tn), (d_ks[kslot], ksent[kslot]),
                                    (d_bias, bias_cnt[0])], inc=s_v)
                    kevac_last = s_v.n
                    vn = s_v.n
                    plan.add("sync",
                             lambda eng, L=L, kv=kv, kslot=kslot:
                             eng.dma_start(out=cc_in[L][kv * 128:(kv + 1) * 128, :],
                                           in_=kown[:, kslot * 1024:(kslot + 1) * 1024]),
                             waits=[(s_v, vn)], inc=d_ks[kslot], inc_by=16)
                    ksent[kslot] = d_ks[kslot].n
                plan.flush()

            with ExitStack() as stp:
                wv_bf = stp.enter_context(nc.sbuf_tensor(f"wv_bf{L}", [128, 3 * 512], BF16))
                vown = stp.enter_context(nc.sbuf_tensor(f"vown{L}", [128, 2 * 512], BF16))
                ps_v = stp.enter_context(nc.psum_tensor(f"ps_v{L}", [128, 4096], F32))

                wvu = [0, 0, 0]
                vsent = [0, 0]
                for fh in range(2):
                    for ic in range(32):
                        slot = (fh * 32 + ic) % 3
                        w = [(s_t, wvu[slot])]
                        if fh == 1 and ic == 0:
                            w.append((s_v, s_v.n))
                        plan.add("gpsimd",
                                 lambda eng, Wd=Wd, fh=fh, ic=ic, slot=slot:
                                 eng.dma_start(out=wv_bf[:, slot * 512:(slot + 1) * 512],
                                               in_=Wd["wv"][ic * 128:(ic + 1) * 128, fh * 512:(fh + 1) * 512]),
                                 waits=w, inc=d_wv[slot], inc_by=16)
                        gpn = d_wv[slot].n
                        for tc in range(8):
                            w = [(d_wv[slot], gpn)] if tc == 0 else []
                            if fh == 0 and ic == 0 and tc == 0:
                                w += psum_guard()
                            plan.add("tensor",
                                     lambda eng, ic=ic, tc=tc, slot=slot:
                                     eng.matmul(ps_v[:, tc * 512:(tc + 1) * 512],
                                                x_sb[:, ic * 1024 + tc * 128: ic * 1024 + (tc + 1) * 128],
                                                wv_bf[:, slot * 512:(slot + 1) * 512],
                                                start=(ic == 0), stop=(ic == 31)),
                                     waits=w, inc=(s_t if tc == 7 else None))
                        wvu[slot] = s_t.n
                    tn = s_t.n
                    for tc in range(8):
                        vslot = tc % 2
                        w = [(s_t, tn), (d_vs[vslot], vsent[vslot]),
                             (d_bias, bias_cnt[0])]
                        plan.add("vector",
                                 lambda eng, fh=fh, tc=tc, vslot=vslot:
                                 eng.tensor_tensor(vown[:, vslot * 512:(vslot + 1) * 512],
                                                   ps_v[:, tc * 512:(tc + 1) * 512],
                                                   bv_sb[:, fh * 512:(fh + 1) * 512], ALU.add),
                                 waits=w, inc=s_v)
                        vn = s_v.n
                        plan.add("sync",
                                 lambda eng, L=L, fh=fh, tc=tc, vslot=vslot:
                                 eng.dma_start(out=cc_in[L][1024 + tc * 128: 1024 + (tc + 1) * 128,
                                                            fh * 512:(fh + 1) * 512],
                                               in_=vown[:, vslot * 512:(vslot + 1) * 512]),
                                 waits=[(s_v, vn)], inc=d_vs[vslot], inc_by=16)
                        vsent[vslot] = d_vs[vslot].n
                ccw = [(d_ks[0], d_ks[0].n), (d_ks[1], d_ks[1].n),
                       (d_vs[0], d_vs[0].n), (d_vs[1], d_vs[1].n)]
                plan.add("gpsimd",
                         lambda eng, L=L:
                         eng.collective_compute("AllGather", ALU.bypass,
                                                replica_groups=[[0, 1], [2, 3], [4, 5], [6, 7]],
                                                ins=[cc_in[L][:, :]], outs=[cc_out[L][:, :]]),
                         waits=ccw, inc=s_cc[L])
                plan.flush()

            with ExitStack() as stp:
                kt = stp.enter_context(nc.sbuf_tensor(f"kt{L}", [128, 2 * 2048], BF16))
                vt = stp.enter_context(nc.sbuf_tensor(f"vt{L}", [128, 2 * 2048], BF16))
                q_bf = stp.enter_context(nc.sbuf_tensor(f"q_bf{L}", [128, 2 * 512], BF16))
                pt = stp.enter_context(nc.sbuf_tensor(f"pt{L}", [128, 3 * 1024], BF16))
                et = stp.enter_context(nc.sbuf_tensor(f"et{L}", [128, 1024], BF16))
                wq_bf = stp.enter_context(nc.sbuf_tensor(f"wq_bf{L}", [128, 3 * 1024], BF16))
                wo_bf = stp.enter_context(nc.sbuf_tensor(f"wo_bf{L}", [128, 3 * 2048], BF16))
                estrip = (stp.enter_context(nc.sbuf_tensor("estrip1", [128, ESTRIP_W], BF16))
                          if L == 1 else None)

                stitch_done = [0, 0]
                wq_used = [0, 0, 0]
                wqctr = [0]
                est_used = 0
                bc_mm = [0]
                hdv = [0]
                av_done = [0]
                qread = [0, 0]
                PTR = 3
                MAXPEND = 2
                for H in range(2):
                    with ExitStack() as stq:
                        ps_q = [stq.enter_context(nc.psum_tensor(f"ps_q{L}{H}{b}", [128, 512], F32))
                                for b in range(2)]
                        ps_s = [stq.enter_context(nc.psum_tensor(f"ps_s{L}{H}{r}", [128, 1024], F32))
                                for r in range(2)]
                        ps_o = stq.enter_context(nc.psum_tensor(f"ps_o{L}{H}", [128, 512], F32))
                        ps_l = stq.enter_context(nc.psum_tensor(f"ps_l{L}{H}", [1, 512], F32))

                        sgrp_exp = [0, 0]
                        psg = [0, 0]
                        pt_done = [0] * PTR
                        et_free = 0
                        ring = 0
                        ptring = 0
                        qev = [0, 0]
                        guard = psum_guard()
                        pend = []

                        def emit_qproj(t):
                            qb = t % 2
                            for ich in range(4):
                                wqs = wqctr[0] % 3
                                wqctr[0] += 1
                                plan.add("gpsimd",
                                         lambda eng, Wd=Wd, t=t, ich=ich, wqs=wqs:
                                         eng.dma_start(out=wq_bf[:, wqs * 1024:(wqs + 1) * 1024],
                                                       in_=Wd["wq"][t, :, ich * 1024:(ich + 1) * 1024]),
                                         waits=[(s_t, wq_used[wqs])], inc=d_wq[wqs], inc_by=16)
                                gpn = d_wq[wqs].n
                                for icl in range(8):
                                    ic = ich * 8 + icl
                                    w = [(d_wq[wqs], gpn)] if icl == 0 else []
                                    if ic == 0:
                                        w += guard
                                        w.append((s_v, qev[qb]))
                                    plan.add("tensor",
                                             lambda eng, ic=ic, wqs=wqs, icl=icl, H=H, qb=qb:
                                             eng.matmul(ps_q[qb][:],
                                                        wq_bf[:, wqs * 1024 + icl * 128: wqs * 1024 + (icl + 1) * 128],
                                                        x_sb[:, ic * 1024 + H * 512: ic * 1024 + H * 512 + 512],
                                                        start=(ic == 0), stop=(ic == 31)),
                                             waits=w, inc=(s_t if icl == 7 else None))
                                wq_used[wqs] = s_t.n
                                if pend:
                                    pend.pop(0)()
                            while pend:
                                pend.pop(0)()
                            qmm_n = s_t.n
                            plan.add("vector",
                                     lambda eng, t=t, qb=qb:
                                     eng.tensor_scalar(q_bf[:, qb * 512:(qb + 1) * 512],
                                                       ps_q[qb][:],
                                                       bias_sb[:, BQ + t:BQ + t + 1],
                                                       SCALE, ALU.add, ALU.mult),
                                     waits=[(s_t, max(qmm_n, qread[qb])),
                                            (d_bias, bias_cnt[0])], inc=s_v)
                            qev[qb] = s_v.n

                        for kv in range(NKV):
                            kslot = kv % 2
                            w = [(s_cc[L], 1), (s_t, stitch_done[kslot])]
                            for s in range(8):
                                plan.add("sync",
                                         lambda eng, L=L, kv=kv, s=s, kslot=kslot:
                                         eng.dma_start(
                                             out=kt[:, kslot * 2048 + s * 256: kslot * 2048 + (s + 1) * 256],
                                             in_=cc_out[L][2048 * (s % 2) + kv * 128:
                                                           2048 * (s % 2) + (kv + 1) * 128,
                                                           256 * (s // 2): 256 * (s // 2) + 256]),
                                         waits=w, inc=d_st[kslot], inc_by=16)
                                w = []
                            for tc in range(16):
                                s = tc // 2
                                ro = 2048 * (s % 2) + 1024 + 256 * (s // 2) + 128 * (tc % 2)
                                plan.add("sync",
                                         lambda eng, L=L, kv=kv, tc=tc, ro=ro, kslot=kslot:
                                         eng.dma_start(
                                             out=vt[:, kslot * 2048 + tc * 128: kslot * 2048 + (tc + 1) * 128],
                                             in_=cc_out[L][ro:ro + 128, kv * 128:(kv + 1) * 128]),
                                         inc=d_st[kslot], inc_by=16)
                            stitch_n = d_st[kslot].n

                            for g in range(4):
                                h = kv * 4 + g
                                hslot = h % 2
                                if L == 1:
                                    plan.add("sync",
                                             lambda eng, h=h:
                                             eng.dma_start(out=estrip[:, :],
                                                           in_=estripc[h, :, :]),
                                             waits=[(s_v, est_used)], inc=d_es, inc_by=16)
                                estrip_n = d_es.n

                                if h == 0:
                                    emit_qproj(0)
                                if h < 31:
                                    emit_qproj(h + 1)

                                for il in range(2):
                                    i = 2 * H + il
                                    for grp in range(i + 1):
                                        r = ring
                                        ring ^= 1
                                        for j in range(4):
                                            kc = 4 * grp + j
                                            w = []
                                            if j == 0:
                                                w = [(s_sc, sgrp_exp[r]),
                                                     (s_v, max(qev[hslot], psg[r])),
                                                     (d_st[kslot], stitch_n)]
                                            plan.add("tensor",
                                                     lambda eng, kc=kc, j=j, r=r, kslot=kslot, hslot=hslot, il=il:
                                                     eng.matmul(ps_s[r][:, j * 256:(j + 1) * 256],
                                                                kt[:, kslot * 2048 + kc * 128: kslot * 2048 + (kc + 1) * 128],
                                                                q_bf[:, hslot * 512 + il * 256: hslot * 512 + (il + 1) * 256],
                                                                start=True, stop=True),
                                                     waits=w, inc=(s_t if j == 3 else None))
                                        smm_n = s_t.n
                                        masked = (L == 1) or (grp == i)
                                        pr = ptring
                                        ptring = (ptring + 1) % PTR
                                        if masked:
                                            plan.add("scalar",
                                                     lambda eng, r=r:
                                                     eng.activation(et[:], ps_s[r][:], ACT.Exp,
                                                                    bias=EXP_OFF, scale=1.0),
                                                     waits=[(s_t, smm_n), (s_v, et_free)], inc=s_sc)
                                            exp_n = s_sc.n
                                            sgrp_exp[r] = exp_n
                                            for j in range(4):
                                                w = [(s_sc, exp_n)]
                                                if j == 0:
                                                    w.append((s_t, pt_done[pr]))
                                                    if L == 1:
                                                        w.append((d_es, estrip_n))
                                                    else:
                                                        w.append((d_ms, d_ms.n))
                                                if L == 0:
                                                    def mkb(j=j, pr=pr):
                                                        def b(eng):
                                                            return eng.tensor_mul(
                                                                pt[:, pr * 1024 + j * 256: pr * 1024 + (j + 1) * 256],
                                                                et[:, j * 256:(j + 1) * 256],
                                                                mask_sb[:, 384 - 128 * j: 640 - 128 * j])
                                                        return b
                                                else:
                                                    m = 512 * (i - grp) - 128 * j + 384

                                                    def mkb(j=j, pr=pr, m=m):
                                                        def b(eng):
                                                            return eng.tensor_mul(
                                                                pt[:, pr * 1024 + j * 256: pr * 1024 + (j + 1) * 256],
                                                                et[:, j * 256:(j + 1) * 256],
                                                                estrip[:, m: m + 256])
                                                        return b
                                                plan.add("vector", mkb(), waits=w,
                                                         inc=(s_v if j == 3 else None))
                                            et_free = s_v.n
                                            pt_ready = (s_v, s_v.n)
                                        else:
                                            plan.add("scalar",
                                                     lambda eng, r=r, pr=pr:
                                                     eng.activation(pt[:, pr * 1024:(pr + 1) * 1024],
                                                                    ps_s[r][:], ACT.Exp,
                                                                    bias=EXP_OFF, scale=1.0),
                                                     waits=[(s_t, max(smm_n, pt_done[pr]))], inc=s_sc)
                                            sgrp_exp[r] = s_sc.n
                                            pt_ready = (s_sc, s_sc.n)

                                        def mkC(pr=pr, kslot=kslot, il=il, grp=grp, i=i, pt_ready=pt_ready):
                                            def C():
                                                for j in range(4):
                                                    kc = 4 * grp + j
                                                    first = (kc == 0)
                                                    last = (kc == 4 * i + 3)
                                                    w = []
                                                    if j == 0:
                                                        w = [pt_ready]
                                                        if first:
                                                            w.append((s_v, hdv[0]))
                                                    plan.add("tensor",
                                                             lambda eng, kc=kc, j=j, pr=pr, kslot=kslot, il=il, first=first, last=last:
                                                             eng.matmul(ps_o[:, il * 256:(il + 1) * 256],
                                                                        vt[:, kslot * 2048 + kc * 128: kslot * 2048 + (kc + 1) * 128],
                                                                        pt[:, pr * 1024 + j * 256: pr * 1024 + (j + 1) * 256],
                                                                        start=first, stop=last),
                                                             waits=w)
                                                    plan.add("tensor",
                                                             lambda eng, j=j, pr=pr, il=il, first=first, last=last:
                                                             eng.matmul(ps_l[:, il * 256:(il + 1) * 256],
                                                                        ones128[:],
                                                                        pt[:, pr * 1024 + j * 256: pr * 1024 + (j + 1) * 256],
                                                                        start=first, stop=last),
                                                             inc=(s_t if j == 3 else None))
                                                pt_done[pr] = s_t.n
                                                av_done[0] = s_t.n
                                            return C
                                        pend.append(mkC())
                                        while len(pend) > MAXPEND:
                                            pend.pop(0)()
                                qread[hslot] = s_t.n
                                if L == 1:
                                    est_used = et_free

                                def mkD(h=h, kslot=kslot, ps_o=ps_o, ps_l=ps_l):
                                    st = {}

                                    def D1():
                                        av_n = av_done[0]
                                        stitch_done[kslot] = av_n
                                        st["sX"] = ring ^ 1
                                        plan.add("vector",
                                                 lambda eng: eng.reciprocal(ln_sb[:, 0:512], ps_l[:]),
                                                 waits=[(s_t, max(av_n, bc_mm[0]))], inc=s_v)
                                        plan.add("vector",
                                                 lambda eng: eng.tensor_copy(rl_bf[:], ln_sb[:, 0:512]),
                                                 inc=s_v)
                                        st["rl_n"] = s_v.n

                                    def D2():
                                        sX = st["sX"]
                                        plan.add("tensor",
                                                 lambda eng, sX=sX: eng.matmul(ps_s[sX][:, 0:512],
                                                                               ones1b[:], rl_bf[:],
                                                                               start=True, stop=True),
                                                 waits=[(s_v, st["rl_n"]), (s_sc, sgrp_exp[sX])], inc=s_t)
                                        bc_mm[0] = s_t.n
                                        plan.add("vector",
                                                 lambda eng, sX=sX: eng.tensor_copy(bc_sb[:], ps_s[sX][:, 0:512]),
                                                 waits=[(s_t, bc_mm[0])], inc=s_v)
                                        psg[sX] = s_v.n
                                        plan.add("vector",
                                                 lambda eng, h=h:
                                                 eng.tensor_mul(attn_buf[:, h * 512:(h + 1) * 512],
                                                                ps_o[:], bc_sb[:]),
                                                 inc=s_v)
                                        hdv[0] = s_v.n
                                    return D1, D2
                                d1, d2 = mkD()
                                pend.append(d1)
                                pend.append(d2)
                        for f in pend:
                            f()
                        pend = []
                        attn_done = hdv[0]
                        if sync_level >= 1:
                            barrier()

                    with ExitStack() as stw:
                        ps_w = stw.enter_context(nc.psum_tensor(f"ps_w{L}{H}", [128, 2048], F32))
                        ps_st = stw.enter_context(nc.psum_tensor(f"ps_st{L}{H}", [1, 1024], F32))
                        ps_ab = stw.enter_context(nc.psum_tensor(f"ps_ab{L}{H}", [128, 1024], F32))

                        wo_used = [0, 0, 0]
                        evac_v = 0
                        pend_ep = []
                        for ocg in range(8):
                            for jcb in range(8):
                                wslot = (ocg * 8 + jcb) % 3
                                plan.add("gpsimd",
                                         lambda eng, Wd=Wd, ocg=ocg, jcb=jcb, wslot=wslot:
                                         eng.dma_start(out=wo_bf[:, wslot * 2048:(wslot + 1) * 2048],
                                                       in_=Wd["wo"][ocg, jcb, :, :]),
                                         waits=[(s_t, wo_used[wslot])], inc=d_wo[wslot], inc_by=16)
                                gpn = d_wo[wslot].n
                                for jcl in range(4):
                                    for ocl in range(4):
                                        first = (jcb == 0 and jcl == 0)
                                        last = (jcb == 7 and jcl == 3)
                                        w = []
                                        if jcl == 0 and ocl == 0:
                                            w.append((d_wo[wslot], gpn))
                                        if first and ocl == 0:
                                            w.append((s_v, max(attn_done, evac_v)))
                                            if ocg == 0:
                                                w += psum_guard()
                                        plan.add("tensor",
                                                 lambda eng, jcb=jcb, jcl=jcl, ocl=ocl, wslot=wslot, first=first, last=last:
                                                 eng.matmul(ps_w[:, ocl * 512:(ocl + 1) * 512],
                                                            wo_bf[:, wslot * 2048 + (jcl * 4 + ocl) * 128:
                                                                  wslot * 2048 + (jcl * 4 + ocl + 1) * 128],
                                                            attn_buf[:, (jcb * 4 + jcl) * 512:(jcb * 4 + jcl + 1) * 512],
                                                            start=first, stop=last),
                                                 waits=w,
                                                 inc=(s_t if (jcl == 3 and ocl == 3) else None))
                                wo_used[wslot] = s_t.n
                            for f in pend_ep:
                                f()
                            pend_ep = []
                            for ocl in range(4):
                                oc = ocg * 4 + ocl
                                evac_v, rows = emit_epilogue(ps_w, ocl * 512, BO + oc, oc, H,
                                                             ps_st, 0, oc == 0, oc == 31, t_buf,
                                                             slot=ocl)
                                pend_ep.append(rows)
                        for f in pend_ep:
                            f()
                        pend_ep = []
                        if sync_level >= 1:
                            barrier()
                        emit_ln(H, ps_st, 0, ps_ab, subidx_attn, t_buf)
                        if sync_level >= 1:
                            barrier()
                        plan.flush()

            with ExitStack() as stf0:
                ps_st2 = stf0.enter_context(nc.psum_tensor(f"ps_st2{L}", [1, 2048], F32))
                if L == 0:
                    qt_sb = stf0.enter_context(nc.sbuf_tensor("qt_sb", [128, 3 * 1024], I32))
                    st_sb = stf0.enter_context(nc.sbuf_tensor("st_sb", [128, 3 * 1024], F32))
                    wd_bf = stf0.enter_context(nc.sbuf_tensor("wd_bf", [128, 2 * 256], BF16))
                else:
                    a_bf = stf0.enter_context(nc.sbuf_tensor("a_bf", [128, 2 * 1024], BF16))
                    f_sb = stf0.enter_context(nc.sbuf_tensor("f_sb", [128, 4 * 1024], BF16))
                    with ExitStack() as stfd:
                        ps_fold = stfd.enter_context(nc.psum_tensor("ps_fold", [128, 1024], F32))
                        fold_ev = 0
                        for fc in range(4):
                            for m in range(8):
                                for half in range(2):
                                    w = []
                                    if m == 0 and half == 0:
                                        w = psum_guard() + [(s_v, fold_ev)]
                                        if fc == 0:
                                            w.append((d_ms, d_ms.n))
                                    plan.add("tensor",
                                             lambda eng, fc=fc, m=m, half=half:
                                             eng.matmul(ps_fold[:, half * 512:(half + 1) * 512],
                                                        ident[:],
                                                        x_sb[:, (fc + 4 * m) * 1024 + half * 512:
                                                             (fc + 4 * m) * 1024 + half * 512 + 512],
                                                        start=(m == 0), stop=(m == 7)),
                                             waits=w, inc=(s_t if (m == 7 and half == 1) else None))
                            tn = s_t.n
                            plan.add("vector",
                                     lambda eng, fc=fc:
                                     eng.tensor_copy(f_sb[:, fc * 1024:(fc + 1) * 1024], ps_fold[:]),
                                     waits=[(s_t, tn)], inc=s_v)
                            fold_ev = s_v.n

                with ExitStack() as stf:
                    ps_f = stf.enter_context(nc.psum_tensor(f"ps_f{L}", [128, 2048], F32))
                    evac_f = 0
                    qt_used = [0, 0, 0]
                    wd_used = [0, 0]
                    a_used = [0, 0]
                    pend_epf = []
                    for og in range(16):
                        if L == 0:
                            for icg in range(8):
                                islot = (og * 8 + icg) % 3
                                plan.add("sync",
                                         lambda eng, og=og, icg=icg, islot=islot:
                                         eng.dma_start(out=qt_sb[:, islot * 1024:(islot + 1) * 1024],
                                                       in_=qt0[og, :, icg * 1024:(icg + 1) * 1024]),
                                         waits=[(s_v, qt_used[islot])], inc=d_qt[islot], inc_by=16)
                                plan.add("gpsimd",
                                         lambda eng, og=og, icg=icg, islot=islot:
                                         eng.dma_start(out=st_sb[:, islot * 1024:(islot + 1) * 1024],
                                                       in_=st0[og, :, icg * 1024:(icg + 1) * 1024]),
                                         waits=[(s_v, qt_used[islot])], inc=d_sq[islot], inc_by=16)
                                dman = d_qt[islot].n
                                sqn = d_sq[islot].n
                                for icl in range(4):
                                    ic = icg * 4 + icl
                                    icol = islot * 1024 + icl * 256
                                    wds = ic % 2
                                    plan.add("vector",
                                             lambda eng, icol=icol, wds=wds:
                                             eng.scalar_tensor_tensor(wd_bf[:, wds * 256:(wds + 1) * 256],
                                                                      qt_sb[:, icol:icol + 256],
                                                                      -128.0,
                                                                      st_sb[:, icol:icol + 256],
                                                                      ALU.add, ALU.mult),
                                             waits=[(d_qt[islot], dman), (d_sq[islot], sqn),
                                                    (s_t, wd_used[wds])], inc=s_v)
                                    if icl == 3:
                                        qt_used[islot] = s_v.n
                                    wdn = s_v.n
                                    for osc in range(2):
                                        for half in range(2):
                                            w = []
                                            if osc == 0 and half == 0:
                                                w.append((s_v, wdn))
                                                if ic == 0:
                                                    w.append((s_v, evac_f))
                                                    if og == 0:
                                                        w += psum_guard()
                                            plan.add("tensor",
                                                     lambda eng, ic=ic, osc=osc, half=half, wds=wds:
                                                     eng.matmul(ps_f[:, (osc * 2 + half) * 512:(osc * 2 + half + 1) * 512],
                                                                wd_bf[:, wds * 256 + osc * 128: wds * 256 + (osc + 1) * 128],
                                                                x_sb[:, ic * 1024 + half * 512: ic * 1024 + half * 512 + 512],
                                                                start=(ic == 0), stop=(ic == 31)),
                                                     waits=w,
                                                     inc=(s_t if (osc == 1 and half == 1) else None))
                                    wd_used[wds] = s_t.n
                        else:
                            aslot = og % 2
                            plan.add("gpsimd",
                                     lambda eng, og=og, aslot=aslot:
                                     eng.dma_start(out=a_bf[:, aslot * 1024:(aslot + 1) * 1024],
                                                   in_=a1[og, :, :]),
                                     waits=[(s_t, a_used[aslot])], inc=d_a[aslot], inc_by=16)
                            gpn = d_a[aslot].n
                            for fc in range(4):
                                for osc in range(2):
                                    for half in range(2):
                                        w = []
                                        if fc == 0 and osc == 0 and half == 0:
                                            w.append((d_a[aslot], gpn))
                                            w.append((s_v, max(evac_f, fold_ev)))
                                            if og == 0:
                                                w += psum_guard()
                                        plan.add("tensor",
                                                 lambda eng, fc=fc, osc=osc, half=half, aslot=aslot:
                                                 eng.matmul(ps_f[:, (osc * 2 + half) * 512:(osc * 2 + half + 1) * 512],
                                                            a_bf[:, aslot * 1024 + fc * 256 + osc * 128:
                                                                 aslot * 1024 + fc * 256 + (osc + 1) * 128],
                                                            f_sb[:, fc * 1024 + half * 512: fc * 1024 + half * 512 + 512],
                                                            start=(fc == 0), stop=(fc == 3)),
                                                 waits=w,
                                                 inc=(s_t if (fc == 3 and osc == 1 and half == 1) else None))
                            a_used[aslot] = s_t.n
                        for f in pend_epf:
                            f()
                        pend_epf = []
                        for osc in range(2):
                            for half in range(2):
                                oc = og * 2 + osc
                                tsrc = t_buf if half == 0 else attn_buf
                                evac_f, rows = emit_epilogue(ps_f, (osc * 2 + half) * 512, FB + oc, oc, half,
                                                             ps_st2, half * 1024,
                                                             first=(og == 0 and osc == 0),
                                                             last=(og == 15 and osc == 1),
                                                             tsrc=tsrc, slot=osc * 2 + half)
                                pend_epf.append(rows)
                    for f in pend_epf:
                        f()
                    pend_epf = []
                with ExitStack() as stl:
                    ps_ab2 = stl.enter_context(nc.psum_tensor(f"ps_ab2{L}", [128, 1024], F32))
                    if sync_level >= 1:
                        barrier()
                    emit_ln(0, ps_st2, 0, ps_ab2, subidx_ffn, t_buf)
                    emit_ln(1, ps_st2, 1024, ps_ab2, subidx_ffn, attn_buf)
                    if sync_level >= 1:
                        barrier()
                    plan.flush()

        plan.add("sync", None, waits=[(d_y[0], d_y[0].n), (d_y[1], d_y[1].n)])
        barrier()
        plan.flush()

    return nc


_CACHE = {}


def _lhsT_blocks(Wm):
    I_, F_ = Wm.shape
    return np.ascontiguousarray(
        Wm.reshape(I_ // 128, 128, F_ // 128, 128).transpose(2, 1, 0, 3).reshape(F_ // 128, 128, (I_ // 128) * 128))


def _wo_blocks(Wm):
    r = Wm.reshape(8, 4, 128, 8, 4, 128)
    return np.ascontiguousarray(r.transpose(3, 0, 2, 1, 4, 5).reshape(8, 8, 128, 2048))


def _qt_blocks(qf):
    return np.ascontiguousarray(
        qf.reshape(16, 256, 32, 128).transpose(0, 3, 2, 1).reshape(16, 128, 8192))


def _a_blocks(A):
    return np.ascontiguousarray(
        A.reshape(4, 128, 16, 256).transpose(2, 1, 0, 3).reshape(16, 128, 1024))


def _bias_cols(b):
    return np.ascontiguousarray(b.reshape(-1, 128).T)


def _strips(h):
    p = np.arange(128)[:, None]
    cm = np.arange(MASK_W)[None, :]
    relm = cm - 384 + 256 * h - p
    mask = (relm >= 0).astype(ml_dtypes.bfloat16)

    ce = np.arange(ESTRIP_W)[None, :]
    rele = ce - 384 + 256 * h - p
    base = 2.0 ** (-8.0 / 32.0)
    slopes = (base ** np.arange(1, 33, dtype=np.float32)).astype(np.float32)
    rel_pos = np.maximum(rele, 0).astype(np.float64)
    sq = np.sqrt(rel_pos)
    est = np.empty((NH, 128, ESTRIP_W), dtype=ml_dtypes.bfloat16)
    for hh in range(NH):
        e = np.where(rele >= 0, np.exp(-float(slopes[hh]) * sq), 0.0)
        est[hh] = e.astype(ml_dtypes.bfloat16)
    return mask, est


def _prep_shared(inputs):
    f32 = np.float32
    sh = {}
    sh["wq0"] = _lhsT_blocks(np.asarray(inputs["L0_attn_Wq"], f32))
    sh["wk0"] = _lhsT_blocks(np.asarray(inputs["L0_attn_Wk"], f32))
    sh["wv0"] = np.ascontiguousarray(np.asarray(inputs["L0_attn_Wv"], f32))
    sh["wo0"] = _wo_blocks(np.asarray(inputs["L0_attn_Wo"], f32))
    sh["bq0"] = _bias_cols(np.asarray(inputs["L0_attn_bq"], f32))
    sh["bk0"] = _bias_cols(np.asarray(inputs["L0_attn_bk"], f32))
    sh["bo0"] = _bias_cols(np.asarray(inputs["L0_attn_bo"], f32))
    sh["fb0"] = _bias_cols(np.asarray(inputs["L0_ffn_bias"], f32))
    sh["bv0"] = np.ascontiguousarray(
        np.broadcast_to(np.asarray(inputs["L0_attn_bv"], f32), (128, 1024)))
    qf = np.asarray(inputs["L0_ffn_q_vals"], np.int32).reshape(4096, 4096)
    sh["qt0"] = _qt_blocks(qf)
    sexp = np.repeat(np.asarray(inputs["L0_ffn_q_scales"], f32), 32, axis=1)
    sh["st0"] = _qt_blocks(sexp)
    wqkv = np.asarray(inputs["L1_attn_w_qkv"], f32)
    sh["wq1"] = _lhsT_blocks(np.ascontiguousarray(wqkv[:, :4096]))
    sh["wk1"] = _lhsT_blocks(np.ascontiguousarray(wqkv[:, 4096:5120]))
    sh["wv1"] = np.ascontiguousarray(wqkv[:, 5120:6144])
    sh["wo1"] = _wo_blocks(np.asarray(inputs["L1_attn_w_o"], f32))
    z32 = np.zeros((128, 32), f32)
    z8 = np.zeros((128, 8), f32)
    sh["bq1"] = z32
    sh["bk1"] = z8
    sh["bo1"] = z32
    sh["fb1"] = z32
    sh["bv1"] = np.zeros((128, 1024), f32)
    A = np.asarray(inputs["L1_ffn_kernel"], np.int8).reshape(512, 4096)
    sh["a1"] = _a_blocks(A)
    sh["identc"] = np.eye(128, dtype=ml_dtypes.bfloat16)
    return sh


def _own_tokens(h):
    return np.concatenate([np.arange(512 * i + 256 * h, 512 * i + 256 * h + 256) for i in range(4)])


def kernel(**inputs):
    key = ("nc", TAP, SYNC)
    if key not in _CACHE:
        _CACHE.clear()
        _CACHE[key] = build_program(tap=TAP, sync_level=SYNC)
    nc = _CACHE[key]

    sh = _prep_shared(inputs)
    x_full = np.asarray(inputs["L0_attn_x"], np.float32)

    strips = {h: _strips(h) for h in range(2)}
    in_maps = []
    for c in range(8):
        b, h = c // 2, c % 2
        idx = _own_tokens(h)
        m = dict(sh)
        m["maskc"] = strips[h][0]
        m["estripc"] = strips[h][1]
        m["x0"] = np.ascontiguousarray(x_full[b][idx].T)
        in_maps.append(m)

    res = run_bass_kernel_spmd(nc, in_maps, core_ids=list(range(8)))

    out = np.empty((4, 2048, 4096), np.float32)
    for c in range(8):
        b, h = c // 2, c % 2
        idx = _own_tokens(h)
        out[b, idx, :] = res.results[c]["y"].T
    return out
